# version 16
# baseline (speedup 1.0000x reference)
import math
import os
import numpy as np
import ml_dtypes
from contextlib import ExitStack

import concourse.bass as bass
import concourse.mybir as mybir
from concourse.bass_utils import run_bass_kernel_spmd

F32 = mybir.dt.float32
BF16 = mybir.dt.bfloat16
I32 = mybir.dt.int32
U32 = mybir.dt.uint32
AF = mybir.ActivationFunctionType
ALU = mybir.AluOpType
AX = mybir.AxisListType

B, S, H = 2, 4096, 4096
HKV, GQ, D = 8, 4, 128
DKV = HKV * D
NQT = 8
TOK = NQT * 128
SM_SCALE = 1.0 / math.sqrt(D)
NB = 256
NNB = H // NB
EPS = 1e-5

ENGS = ("te", "ve", "se", "gq", "sq")


def gmaj(h):
    return (h % GQ) * HKV + h // GQ


class Prog:

    def __init__(self):
        self.ops = []

    def op(self, eng, build, reads=(), writes=(), dma=False):
        assert eng in ENGS
        self.ops.append((eng, build, tuple(reads), tuple(writes), dma))

    def fence(self):
        self.ops.append(("FENCE", None, (), (), False))

    def emit(self, nc, es, regs_init, final_waits_eng="se"):
        cs = {e: es.enter_context(nc.semaphore(f"cs_{e}")) for e in ENGS}
        ds = {e: es.enter_context(nc.semaphore(f"ds_{e}")) for e in ("gq", "sq", "se")}
        cnt_c = {e: 0 for e in ENGS}
        cnt_d = {e: 0 for e in ("gq", "sq", "se")}
        plan = []
        last_w = {}
        readers = {}
        fence_low = {}
        expanded = []
        for entry in self.ops:
            if entry[0] == "FENCE":
                expanded.append(entry)
            else:
                expanded.append(entry)
        self.ops = expanded
        for (eng, build, reads, writes, dma) in self.ops:
            if eng == "FENCE":
                plan.append(("FENCE", dict((("c", e), cnt_c[e]) for e in ENGS)
                             | dict((("d", e), cnt_d[e]) for e in ("gq", "sq", "se")), None))
                continue
            if dma:
                cnt_d[eng] += 16
                mykey, myval = ("d", eng), cnt_d[eng]
            else:
                cnt_c[eng] += 1
                mykey, myval = ("c", eng), cnt_c[eng]
            need = {}

            def add(dep):
                if dep is None:
                    return
                k, v = dep
                if k == mykey and v == myval:
                    return
                if need.get(k, -1) < v:
                    need[k] = v
            for b in reads:
                add(last_w.get(b))
            for b in writes:
                add(last_w.get(b))
                for dep in readers.get(b, ()):
                    add(dep)
            for b in writes:
                last_w[b] = (mykey, myval)
                readers[b] = []
            for b in reads:
                readers.setdefault(b, []).append((mykey, myval))
            plan.append((mykey, myval, need))

        sem_of = {("c", e): cs[e] for e in ENGS}
        for e in ("gq", "sq", "se"):
            sem_of[("d", e)] = ds[e]
        by_eng = {e: [i for i, o in enumerate(self.ops) if o[0] in (e, "FENCE")] for e in ENGS}
        ops = self.ops
        blk = es.enter_context(nc.Block())
        satisfied = {e: {} for e in ENGS}

        def make_body(eng_name):
            idxs = by_eng[eng_name]

            def body(eh):
                if eng_name == "gq" and regs_init is not None:
                    regs_init(eh)
                sat = satisfied[eng_name]
                for i in idxs:
                    if ops[i][0] == "FENCE":
                        totals = plan[i][1]
                        for k in sorted(totals, key=repr):
                            v = totals[k]
                            if v <= 0 or sat.get(k, 0) >= v:
                                continue
                            eh.wait_ge(sem_of[k], v)
                            sat[k] = v
                        continue
                    _, build, _, _, dma = ops[i]
                    mykey, myval, need = plan[i]
                    for k in sorted(need, key=repr):
                        v = need[k]
                        if sat.get(k, 0) >= v:
                            continue
                        eh.wait_ge(sem_of[k], v)
                        sat[k] = v
                    inst = build(eh)
                    if inst is None:
                        inst = eh.nop()
                    inst.then_inc(sem_of[mykey], 16 if dma else 1)
                if eng_name == final_waits_eng:
                    for e in ("se", "sq", "gq"):
                        if cnt_d[e] > 0:
                            eh.wait_ge(ds[e], cnt_d[e])

            return body

        blk.tensor(make_body("te"))
        blk.vector(make_body("ve"))
        blk.scalar(make_body("se"))
        blk.gpsimd(make_body("gq"))
        blk.sync(make_body("sq"))


def build_program():
    nc = bass.Bass("TRN2", target_bir_lowering=False)

    x_own = nc.dram_tensor("x_own", [TOK, H], F32, kind="ExternalInput")
    x_far = nc.dram_tensor("x_far", [TOK, H], F32, kind="ExternalInput")
    wq = nc.dram_tensor("wq", [H, H], F32, kind="ExternalInput")
    wk = nc.dram_tensor("wk", [H, DKV], F32, kind="ExternalInput")
    wv = nc.dram_tensor("wv", [H, DKV], F32, kind="ExternalInput")
    wo = nc.dram_tensor("wo", [H, H], F32, kind="ExternalInput")
    wT = nc.dram_tensor("wT", [H, H], F32, kind="ExternalInput")
    gwT = nc.dram_tensor("gwT", [H, 8], F32, kind="ExternalInput")
    bias1 = nc.dram_tensor("bias1", [1, H], F32, kind="ExternalInput")
    incs = nc.dram_tensor("incs", [8, H], F32, kind="ExternalInput")
    outcs = nc.dram_tensor("outcs", [8, H], F32, kind="ExternalInput")
    wmask = nc.dram_tensor("wmask", [128, 512], BF16, kind="ExternalInput")
    ident_in = nc.dram_tensor("ident_in", [128, 128], BF16, kind="ExternalInput")
    nonce_in = nc.dram_tensor("nonce_in", [1, 4], I32, kind="ExternalInput")
    out_d = nc.dram_tensor("out_d", [TOK, H], F32, kind="ExternalOutput")

    xn_own_d = nc.dram_tensor("xn_own_d", [TOK, H], BF16)
    xn_far_d = nc.dram_tensor("xn_far_d", [TOK, H], BF16)
    xs_d = nc.dram_tensor("xs_d", [H, TOK], BF16)
    attn_d = nc.dram_tensor("attn_d", [TOK, H], BF16)
    wTs_d = nc.dram_tensor("wTs_d", [H, H], BF16)
    kvK = nc.dram_tensor("kvK", [S, 1024], BF16, addr_space="Shared")
    kvK_st = nc.dram_tensor("kvK_st", [1024, 1024], BF16)
    kvV_st = nc.dram_tensor("kvV_st", [1024, 1024], BF16)
    kvV = nc.dram_tensor("kvV", [S, DKV], BF16, addr_space="Shared")
    meta_n = nc.dram_tensor("meta_n", [2, 512], F32, addr_space="Shared")
    meta_f = nc.dram_tensor("meta_f", [2, 128], I32, addr_space="Shared")

    P = Prog()
    es = ExitStack()
    SB = lambda name, shape, dt: es.enter_context(nc.sbuf_tensor(name, shape, dt))

    Qfm = [SB(f"Qfm{m}", [128, TOK], BF16) for m in range(32)]
    mask_sb = SB("mask_sb", [128, 512], BF16)
    id_sb = SB("id_sb", [128, 128], BF16)
    ones_col = SB("ones_col", [128, 1], BF16)
    ones_row = SB("ones_row", [1, 128], BF16)
    eps_sb = SB("eps_sb", [128, 1], F32)
    lnsc_sb = SB("lnsc_sb", [1, 1], F32)
    routT = SB("routT", [8, TOK], BF16)
    qn2 = SB("qn2", [1, 32], F32)
    kn_mine = SB("kn_mine", [1, 8], F32)
    kn_par = SB("kn_par", [1, 8], F32)
    nred = SB("nred", [1, 2], F32)
    pid_sb = SB("pid_sb", [1, 1], U32)
    nonce_sb = SB("nonce_sb", [1, 4], I32)
    flag_sb = SB("flag_sb", [1, 4], I32)

    def dma(q, out, in_, **kw):
        return lambda eh: eh.dma_start(out=out, in_=in_, **kw)

    P.op("gq", dma("gq", pid_sb[:], nc.partition_id_tensor[:]), writes=["pid"], dma=True)
    P.op("gq", dma("gq", nonce_sb[:], nonce_in[:]), writes=["nonce"], dma=True)
    P.op("sq", dma("sq", mask_sb[:], wmask[:]), writes=["mask"], dma=True)
    P.op("sq", dma("sq", id_sb[:], ident_in[:]), writes=["ident"], dma=True)
    P.op("ve", lambda eh: eh.memset(ones_col[:], 1.0), writes=["ones_col"])
    P.op("ve", lambda eh: eh.memset(ones_row[:], 1.0), writes=["ones_row"])
    P.op("ve", lambda eh: eh.memset(eps_sb[:], EPS), writes=["eps"])
    P.op("ve", lambda eh: eh.memset(lnsc_sb[:], math.log(SM_SCALE)), writes=["lnsc"])

    REG = {}

    def regs_init(g):
        names = ["pid", "r", "p", "s2", "vr0", "vr1",
                 "mn_own", "mn_par", "fl_own", "fl_par", "tmp",
                 "tgt", "nd", "fv", "nonce"]
        for n in names:
            REG[n] = es.enter_context(g.register(f"rg_{n}"))

    def regs_compute(g):
        R = REG
        g.reg_load(R["pid"], pid_sb[:1, :1])
        g.reg_load(R["nonce"], nonce_sb[:1, :1])
        g.reg_mod(R["r"], R["pid"], 4)
        g.reg_mod(R["p"], R["pid"], 2)
        g.reg_add(R["s2"], R["r"], 2)
        g.reg_mod(R["s2"], R["s2"], 4)
        g.reg_mov(R["vr0"], 0); g.reg_add(R["vr0"], R["vr0"], R["r"]); g.reg_mul(R["vr0"], R["vr0"], 1024 * 1024)
        g.reg_mov(R["vr1"], 0); g.reg_add(R["vr1"], R["vr1"], R["s2"]); g.reg_mul(R["vr1"], R["vr1"], 1024 * 1024)
        g.reg_mov(R["mn_own"], 0); g.reg_add(R["mn_own"], R["mn_own"], R["p"]); g.reg_mul(R["mn_own"], R["mn_own"], 512)
        g.reg_mov(R["mn_par"], 512); g.reg_sub(R["mn_par"], R["mn_par"], R["mn_own"])
        g.reg_mov(R["fl_own"], 0); g.reg_add(R["fl_own"], R["fl_own"], R["p"]); g.reg_mul(R["fl_own"], R["fl_own"], 128)
        g.reg_mov(R["fl_par"], 128); g.reg_sub(R["fl_par"], R["fl_par"], R["fl_own"])
        return None

    P.op("gq", regs_compute, reads=["pid", "nonce"], writes=["regs"])

    p0 = ExitStack()
    SB0 = lambda name, shape, dt: p0.enter_context(nc.sbuf_tensor(name, shape, dt))
    xt = [SB0(f"p0_x{j}", [128, H], F32) for j in range(2)]
    xnt = [SB0(f"p0_xn{j}", [128, H], BF16) for j in range(2)]
    st6 = [SB0(f"p0_st6{j}", [128, 48], F32) for j in range(2)]
    mv = [SB0(f"p0_mv{j}", [128, 2], F32) for j in range(2)]
    rstd = [SB0(f"p0_rs{j}", [128, 2], F32) for j in range(2)]
    wts_f = [SB0(f"p0_wtf{j}", [128, 2048], F32) for j in range(2)]
    wts_b = [SB0(f"p0_wtb{j}", [128, 2048], BF16) for j in range(2)]

    for t in range(16):
        src = x_own if t < 8 else x_far
        dstd = xn_own_d if t < 8 else xn_far_d
        dtag = "xnod" if t < 8 else "xnfd"
        row = (t % 8) * 128
        j = t % 2
        P.op("se", dma("se", xt[j][:], src[row:row + 128, :]), writes=[("xt", j)], dma=True)
        for i in range(8):
            P.op("ve", (lambda j=j, i=i: lambda eh: eh.bn_stats(
                out=st6[j][:, i * 6:(i + 1) * 6], in_=xt[j][:, i * 512:(i + 1) * 512]))(),
                reads=[("xt", j)], writes=[("st6", j)])
        P.op("ve", (lambda j=j: lambda eh: eh.bn_aggr(out=mv[j][:], in_=st6[j][:]))(),
             reads=[("st6", j)], writes=[("mv", j)])
        P.op("se", (lambda j=j: lambda eh: eh.activation(
            out=rstd[j][:, 0:1], in_=mv[j][:, 1:2], func=AF.Ln, bias=eps_sb[:, 0:1]))(),
            reads=[("mv", j), "eps"], writes=[("rstd", j)])
        P.op("se", (lambda j=j: lambda eh: eh.activation(
            out=rstd[j][:, 1:2], in_=rstd[j][:, 0:1], func=AF.Exp, scale=-0.5))(),
            reads=[("rstd", j)], writes=[("rstd", j)])
        P.op("ve", (lambda j=j: lambda eh: eh.tensor_scalar(
            out=xnt[j][:], in0=xt[j][:], scalar1=mv[j][:, 0:1], scalar2=rstd[j][:, 1:2],
            op0=ALU.subtract, op1=ALU.mult))(),
            reads=[("xt", j), ("mv", j), ("rstd", j)], writes=[("xnt", j)])
        P.op("se", dma("se", dstd[row:row + 128, :], xnt[j][:]),
             reads=[("xnt", j)], writes=[dtag], dma=True)

    for u in range(64):
        j = u % 2
        r0 = (u // 2) * 128
        c0 = (u % 2) * 2048
        P.op("sq", dma("sq", wts_f[j][:], wT[r0:r0 + 128, c0:c0 + 2048]),
             writes=[("wtsf", j)], dma=True)
        P.op("se", (lambda j=j: lambda eh: eh.sign(out=wts_b[j][:], in_=wts_f[j][:]))(),
             reads=[("wtsf", j)], writes=[("wtsb", j)])
        P.op("sq", dma("sq", wTs_d[r0:r0 + 128, c0:c0 + 2048], wts_b[j][:]),
             reads=[("wtsb", j)], writes=["wTs"], dma=True)
    p0.close()
    P.fence()

    p1 = ExitStack()
    SB1 = lambda name, shape, dt: p1.enter_context(nc.sbuf_tensor(name, shape, dt))
    xnfm = [SB1(f"xnfm{k}", [128, TOK], BF16) for k in range(32)]
    ps1 = ExitStack()
    pj = [ps1.enter_context(nc.psum_tensor(f"pj{i}", [128, 1024], F32)) for i in range(3)]
    pd = ps1.enter_context(nc.psum_tensor("pd", [128, 512], F32))
    peb = ps1.enter_context(nc.psum_tensor("peb", [128, 256], BF16))

    def xn_load(which):
        srcd = xn_own_d if which == "own" else xn_far_d
        srctag = "xnod" if which == "own" else "xnfd"
        for k in range(32):
            P.op("sq", dma("sq", xnfm[k][:], srcd[0:TOK, k * 128:(k + 1) * 128], transpose=True),
                 reads=[srctag], writes=[("xnfm", k)], dma=True)

    xn_load("own")

    WS = {}

    def open_wscope():
        st = ExitStack()
        WS["stack"] = st
        WS["wcol_f"] = [st.enter_context(nc.sbuf_tensor(f"wcf{next(_ctr)}", [128, 32, 128], F32)) for j in range(2)]
        WS["wcol_b"] = [st.enter_context(nc.sbuf_tensor(f"wcb{next(_ctr)}", [128, 32, 128], BF16)) for j in range(2)]
        WS["sqt"] = st.enter_context(nc.sbuf_tensor(f"sqt{next(_ctr)}", [128, 1024], BF16))
        WS["kevac"] = [st.enter_context(nc.sbuf_tensor(f"kev{next(_ctr)}", [128, 1024], BF16)) for j in range(2)]
        return st

    import itertools
    _ctr = itertools.count()

    def load_wcol(wten, ncols, m, j):
        wcol_f, wcol_b = WS["wcol_f"], WS["wcol_b"]
        for piece in range(4):
            k0 = piece * 8
            src = bass.AP(wten, (k0 * 128) * ncols + m * 128,
                          [[ncols, 128], [128 * ncols, 8], [1, 128]])
            P.op("sq", dma("sq", wcol_f[j][:, k0:k0 + 8, :], src),
                 writes=[("wcf", j)], dma=True)
        wb = wcol_b[j]
        P.op("se", (lambda j=j, wb=wb: lambda eh: eh.copy(out=wb[:], in_=WS["wcol_f"][j][:] if False else wb[:]))() if False else
             (lambda j=j: lambda eh: eh.copy(out=WS["wcol_b"][j][:], in_=WS["wcol_f"][j][:]))(),
             reads=[("wcf", j)], writes=[("wcb", j)])

    def col_norm_max(src_bf, srckey, dst_ap, dstkey):
        sqt = WS["sqt"]
        P.op("se", (lambda sqt=sqt, src_bf=src_bf: lambda eh: eh.activation(
            out=sqt[:], in_=src_bf[:], func=AF.Square))(),
            reads=[srckey], writes=["sqt"])
        for half in range(2):
            P.op("te", (lambda half=half, sqt=sqt: lambda eh: eh.matmul(
                out=pd[0:1, 0:512], lhsT=ones_col[:], rhs=sqt[:, half * 512:(half + 1) * 512],
                start=True, stop=True))(),
                reads=["sqt", "ones_col"], writes=["pd"])
            P.op("ve", (lambda half=half: lambda eh: eh.reduce_max(
                out=nred[0:1, half:half + 1], in_=pd[0:1, 0:512], axis=AX.X))(),
                reads=["pd"], writes=["nred"])
        P.op("ve", lambda eh: eh.tensor_max(out=nred[0:1, 0:1], in0=nred[0:1, 0:1],
                                            in1=nred[0:1, 1:2]),
             reads=["nred"], writes=["nred"])
        P.op("ve", (lambda dst_ap=dst_ap: lambda eh: eh.tensor_max(
            out=dst_ap, in0=dst_ap, in1=nred[0:1, 0:1]))(),
            reads=["nred", dstkey], writes=[dstkey])

    P.op("ve", lambda eh: eh.memset(qn2[:], 0.0), writes=["qn2"])
    P.op("ve", lambda eh: eh.memset(kn_mine[:], 0.0), writes=["knm"])

    def q_proj():
        for m in range(32):
            j = m % 2
            load_wcol(wq, H, m, j)
            slot = m % 3
            for k in range(32):
                for half in range(2):
                    P.op("te", (lambda slot=slot, j=j, k=k, half=half: lambda eh: eh.matmul(
                        out=pj[slot][:, half * 512:(half + 1) * 512],
                        lhsT=WS["wcol_b"][j][:, k, :], rhs=xnfm[k][:, half * 512:(half + 1) * 512],
                        start=(k == 0), stop=(k == 31)))(),
                        reads=[("wcb", j), ("xnfm", k)], writes=[("pj", slot)])
            P.op("ve", (lambda m=m, slot=slot: lambda eh: eh.tensor_copy(
                out=Qfm[m][:], in_=pj[slot][:]))(),
                reads=[("pj", slot)], writes=[("Q", m)])
            col_norm_max(Qfm[m], ("Q", m), qn2[0:1, gmaj(m):gmaj(m) + 1], "qn2")

    def k_proj(pass_far):
        creg = "vr1" if pass_far else "vr0"
        sttag = "kvKst_f" if pass_far else "kvKst_o"
        for hk in range(HKV):
            j = hk % 2
            load_wcol(wk, DKV, hk, j)
            slot = hk % 3
            for k in range(32):
                for half in range(2):
                    P.op("te", (lambda slot=slot, j=j, k=k, half=half: lambda eh: eh.matmul(
                        out=pj[slot][:, half * 512:(half + 1) * 512],
                        lhsT=WS["wcol_b"][j][:, k, :], rhs=xnfm[k][:, half * 512:(half + 1) * 512],
                        start=(k == 0), stop=(k == 31)))(),
                        reads=[("wcb", j), ("xnfm", k)], writes=[("pj", slot)])
            je = hk % 2
            kv_t = WS["kevac"][je]
            P.op("ve", (lambda slot=slot, kv_t=kv_t: lambda eh: eh.tensor_copy(
                out=kv_t[:], in_=pj[slot][:]))(),
                reads=[("pj", slot)], writes=[("kev", je)])
            col_norm_max(kv_t, ("kev", je), kn_mine[0:1, hk:hk + 1], "knm")

            P.op("sq", (lambda hk=hk, kv_t=kv_t: lambda eh: eh.dma_start(
                out=kvK_st[hk * 128:(hk + 1) * 128, :], in_=kv_t[:]))(),
                reads=[("kev", je), "kvK_cp"], writes=["kvKst"], dma=True)

        def k_copy(eh, creg=creg):
            g = eh
            dst = bass.AP(kvK, REG[creg], [[1024, 1024], [1, 1024]])
            return g.dma_start(out=dst, in_=kvK_st[:])
        P.op("gq", k_copy, reads=["kvKst", "regs"], writes=["kvK_w", "kvK_cp"], dma=True)

    def v_proj(pass_far):
        vreg = "vr1" if pass_far else "vr0"
        p1b = ExitStack()
        tag = "f" if pass_far else "o"
        wvf = [p1b.enter_context(nc.sbuf_tensor(f"wvf{tag}{j}", [128, 512], F32)) for j in range(2)]
        wvb = p1b.enter_context(nc.sbuf_tensor(f"wvb{tag}", [128, 32, 512], BF16))
        vev8 = [p1b.enter_context(nc.sbuf_tensor(f"vev{tag}{mt}", [128, 1024], BF16)) for mt in range(8)]
        for half in range(2):
            for k in range(32):
                j = k % 2
                P.op("sq", (lambda j=j, k=k, half=half: lambda eh: eh.dma_start(
                    out=wvf[j][:], in_=wv[k * 128:(k + 1) * 128, half * 512:(half + 1) * 512]))(),
                    writes=[("wvf", j)], dma=True)
                P.op("se", (lambda j=j, k=k: lambda eh: eh.copy(out=wvb[:, k, :], in_=wvf[j][:]))(),
                     reads=[("wvf", j)], writes=["wvb"])
            for mt in range(8):
                slot = mt % 3
                for k in range(32):
                    P.op("te", (lambda slot=slot, k=k, mt=mt: lambda eh: eh.matmul(
                        out=pj[slot][:, 0:512],
                        lhsT=xnfm[k][:, mt * 128:(mt + 1) * 128],
                        rhs=wvb[:, k, :],
                        start=(k == 0), stop=(k == 31)))(),
                        reads=["wvb", ("xnfm", k)], writes=[("pj", slot)])
                P.op("ve", (lambda slot=slot, mt=mt, half=half: lambda eh: eh.tensor_copy(
                    out=vev8[mt][:, half * 512:(half + 1) * 512], in_=pj[slot][:, 0:512]))(),
                    reads=[("pj", slot)], writes=[("vev", mt)])
        for mt in range(8):
            P.op("sq", (lambda mt=mt, vt=vev8[mt]: lambda eh: eh.dma_start(
                out=kvV_st[mt * 128:(mt + 1) * 128, :], in_=vt[:]))(),
                reads=[("vev", mt), "kvV_cp"], writes=["kvVst"], dma=True)

        def v_copy(eh, vreg=vreg):
            g = eh
            dst = bass.AP(kvV, REG[vreg], [[1024, 1024], [1, 1024]])
            return g.dma_start(out=dst, in_=kvV_st[:])
        P.op("gq", v_copy, reads=["kvVst", "regs"], writes=["kvV_w", "kvV_cp"], dma=True)
        p1b.close()
        P.fence()

    ws = open_wscope()
    q_proj()
    k_proj(False)
    ws.close()
    P.fence()
    v_proj(False)

    p1c = ExitStack()
    gwb = p1c.enter_context(nc.sbuf_tensor("gwb", [128, 32, 8], BF16))
    rtmp = p1c.enter_context(nc.sbuf_tensor("rtmp", [128, 12], F32))
    rbf = p1c.enter_context(nc.sbuf_tensor("rbf", [128, 8], BF16))
    gsrc = bass.AP(gwT, 0, [[8, 128], [1024, 32], [1, 8]])
    P.op("gq", dma("gq", gwb[:], gsrc, allow_slow_non_contiguous=True), writes=["gwb"], dma=True)
    for m in range(8):
        for k in range(32):
            P.op("te", (lambda m=m, k=k: lambda eh: eh.matmul(
                out=pd[:, 0:8], lhsT=xnfm[k][:, m * 128:(m + 1) * 128], rhs=gwb[:, k, :],
                start=(k == 0), stop=(k == 31)))(),
                reads=["gwb", ("xnfm", k)], writes=["pd"])
        P.op("ve", lambda eh: eh.reduce_max(out=rtmp[:, 8:9], in_=pd[:, 0:8], axis=AX.X,
                                            negate=True),
             reads=["pd"], writes=["rtmp"])
        P.op("se", lambda eh: eh.activation(out=rtmp[:, 0:8], in_=pd[:, 0:8], func=AF.Exp,
                                            bias=rtmp[:, 8:9]),
             reads=["pd", "rtmp"], writes=["rtmp"])
        P.op("ve", lambda eh: eh.reduce_sum(out=rtmp[:, 9:10], in_=rtmp[:, 0:8], axis=AX.X),
             reads=["rtmp"], writes=["rtmp"])
        P.op("ve", lambda eh: eh.reciprocal(out=rtmp[:, 10:11], in_=rtmp[:, 9:10]),
             reads=["rtmp"], writes=["rtmp"])
        P.op("ve", lambda eh: eh.tensor_scalar_mul(out=rbf[:], in0=rtmp[:, 0:8],
                                                   scalar1=rtmp[:, 10:11]),
             reads=["rtmp"], writes=["rbf"])
        P.op("te", lambda eh: eh.transpose(out=peb[0:8, 0:128], in_=rbf[:], identity=id_sb[:]),
             reads=["rbf", "ident"], writes=["peb"])
        P.op("ve", (lambda m=m: lambda eh: eh.tensor_copy(
            out=routT[0:8, m * 128:(m + 1) * 128], in_=peb[0:8, 0:128]))(),
            reads=["peb"], writes=["routT"])

    incs_b = p1c.enter_context(nc.sbuf_tensor("incs_b", [8, H], BF16))
    xst = [p1c.enter_context(nc.sbuf_tensor(f"xst{j}", [128, TOK], BF16)) for j in range(2)]
    P.op("gq", dma("gq", incs_b[:], incs[:]), writes=["incsb"], dma=True)
    for k in range(32):
        slot = k % 3
        for half in range(2):
            P.op("te", (lambda slot=slot, k=k, half=half: lambda eh: eh.matmul(
                out=pj[slot][:, half * 512:(half + 1) * 512],
                lhsT=incs_b[0:8, k * 128:(k + 1) * 128],
                rhs=routT[0:8, half * 512:(half + 1) * 512],
                start=True, stop=True))(),
                reads=["incsb", "routT"], writes=[("pj", slot)])
        j = k % 2
        P.op("ve", (lambda slot=slot, j=j, k=k: lambda eh: eh.tensor_tensor(
            out=xst[j][:], in0=xnfm[k][:], in1=pj[slot][:], op=ALU.mult))(),
            reads=[("pj", slot), ("xnfm", k)], writes=[("xst", j)])
        P.op("se", dma("se", xs_d[k * 128:(k + 1) * 128, :], xst[j][:]),
             reads=[("xst", j)], writes=["xs_d"], dma=True)
    p1c.close()
    P.fence()

    xn_load("far")
    ws = open_wscope()
    k_proj(True)
    ws.close()
    P.fence()
    v_proj(True)
    p1.close()
    ps1.close()

    def meta_write(eh):
        dst = bass.AP(meta_n, REG["mn_own"], [[1, 1], [1, 8]])
        return eh.dma_start(out=dst, in_=kn_mine[0:1, 0:8])
    P.op("gq", meta_write, reads=["knm", "regs", "kvK_w", "kvV_w"], writes=["meta_w"], dma=True)

    def flag_write(eh):
        dst = bass.AP(meta_f, REG["fl_own"], [[1, 1], [1, 1]])
        return eh.dma_start(out=dst, in_=nonce_sb[0:1, 0:1])
    P.op("gq", flag_write, reads=["meta_w", "nonce", "regs"], writes=["flag_w"], dma=True)

    poll_sem = es.enter_context(nc.semaphore("poll_sem"))

    def barrier(eh):
        g = eh
        R = REG
        g.reg_mov(R["tgt"], 16)
        g.reg_mov(R["nd"], 1)
        with g.While(R["nd"]):
            src = bass.AP(meta_f, R["fl_par"], [[1, 1], [1, 1]])
            g.dma_start(out=flag_sb[0:1, 0:1], in_=src).then_inc(poll_sem, 16)
            g.wait_ge(poll_sem, R["tgt"])
            g.reg_add(R["tgt"], R["tgt"], 16)
            g.reg_mov(R["nd"], 0)
            g.reg_load(R["fv"], flag_sb[0:1, 0:1])
            with g.If_ne(R["fv"], R["nonce"]):
                g.reg_mov(R["nd"], 1)
        return None
    P.op("gq", barrier, reads=["flag_w", "regs"], writes=["BARRIER"])

    def par_read(eh):
        src = bass.AP(meta_n, REG["mn_par"], [[1, 1], [1, 8]])
        return eh.dma_start(out=kn_par[0:1, 0:8], in_=src)
    P.op("gq", par_read, reads=["BARRIER", "regs"], writes=["knp"], dma=True)

    P.fence()
    p2 = ExitStack()
    SB2 = lambda name, shape, dt: p2.enter_context(nc.sbuf_tensor(name, shape, dt))
    attn_tm = [SB2(f"atm{i}", [128, H], BF16) for i in range(NQT)]
    Kb = [SB2(f"Kb{j}", [128, S], BF16) for j in range(2)]
    Vb = [SB2(f"Vb{j}", [128, 32, 130], BF16) for j in range(2)]
    probs = [SB2(f"probs{j}", [128, 1024], BF16) for j in range(2)]
    negc = SB2("negc", [128, 32], F32)
    kn_all = SB2("kn_all", [1, 8], F32)
    prodv = SB2("prodv", [1, 32], F32)
    negc_bf = SB2("negc_bf", [1, 32], BF16)
    rinv = SB2("rinv", [128, 8], F32)
    ps2 = ExitStack()
    psc = [ps2.enter_context(nc.psum_tensor(f"psc{j}", [128, 1024], F32)) for j in range(2)]
    ppv = [ps2.enter_context(nc.psum_tensor(f"ppv{j}", [128, 512], F32)) for j in range(3)]
    PV_SLOT = {0: (0, 0), 1: (0, 136), 2: (0, 272), 3: (1, 0), 4: (1, 136), 5: (1, 272),
               6: (2, 0), 7: (2, 136)}

    for j in range(2):
        P.op("ve", (lambda j=j: lambda eh: eh.memset(Vb[j][:, :, 128:130], 0.0))(),
             writes=[("Vb", j)])
        P.op("ve", (lambda j=j: lambda eh: eh.memset(Vb[j][:, :, 128:129], 1.0))(),
             writes=[("Vb", j)])

    P.op("ve", lambda eh: eh.tensor_max(out=kn_all[:], in0=kn_mine[0:1, 0:8],
                                        in1=kn_par[0:1, 0:8]),
         reads=["knm", "knp"], writes=["kn_all"])
    for g_ in range(4):
        P.op("ve", (lambda g_=g_: lambda eh: eh.tensor_mul(
            out=prodv[0:1, g_ * 8:(g_ + 1) * 8], in0=qn2[0:1, g_ * 8:(g_ + 1) * 8],
            in1=kn_all[0:1, 0:8]))(),
            reads=["qn2", "kn_all"], writes=["prodv"])
    P.op("se", lambda eh: eh.activation(out=prodv[:], in_=prodv[:], func=AF.Ln),
         reads=["prodv"], writes=["prodv"])
    P.op("se", lambda eh: eh.activation(out=prodv[:], in_=prodv[:], func=AF.Exp,
                                        scale=0.5, bias=lnsc_sb[0:1, 0:1]),
         reads=["prodv", "lnsc"], writes=["prodv"])
    P.op("ve", lambda eh: eh.tensor_scalar_mul(out=negc_bf[:], in0=prodv[:], scalar1=-1.0),
         reads=["prodv"], writes=["negc_bf"])
    P.op("te", lambda eh: eh.matmul(out=psc[0][:, 0:32], lhsT=ones_row[:], rhs=negc_bf[:],
                                    start=True, stop=True),
         reads=["negc_bf", "ones_row"], writes=[("psc", 0)])
    P.op("ve", lambda eh: eh.tensor_copy(out=negc[:], in_=psc[0][:, 0:32]),
         reads=[("psc", 0)], writes=["negc"])

    for hk in range(HKV):
        j = hk % 2
        for sres in range(4):
            P.op("gq", (lambda hk=hk, j=j, sres=sres: lambda eh: eh.dma_start(
                out=Kb[j][:, sres * 1024:(sres + 1) * 1024],
                in_=kvK[sres * 1024 + hk * 128: sres * 1024 + (hk + 1) * 128, :]))(),
                reads=["BARRIER"], writes=[("Kb", j)], dma=True)
        for quad in range(4):
            srcv = bass.AP(kvV, (quad * 1024) * DKV + hk * 128,
                           [[DKV, 128], [128 * DKV, 8], [1, 128]])
            P.op("gq", (lambda srcv=srcv, j=j, quad=quad: lambda eh: eh.dma_start(
                out=Vb[j][:, quad * 8:(quad + 1) * 8, 0:128], in_=srcv))(),
                reads=["BARRIER"], writes=[("Vb", j)], dma=True)
        for gg in range(GQ):
            h = hk * GQ + gg
            for v in range(32):
                i0 = v % 8
                N = (8 - i0) * 128
                jp = v % 2
                for half in range((N + 511) // 512):
                    w = min(512, N - half * 512)
                    P.op("te", (lambda jp=jp, j=j, v=v, h=h, i0=i0, half=half, w=w:
                                lambda eh: eh.matmul(
                        out=psc[jp][:, half * 512:half * 512 + w],
                        lhsT=Kb[j][:, v * 128:(v + 1) * 128],
                        rhs=Qfm[h][:, i0 * 128 + half * 512: i0 * 128 + half * 512 + w],
                        start=True, stop=True))(),
                        reads=[("Kb", j), ("Q", h)], writes=[("psc", jp)])
                P.op("se", (lambda jp=jp, N=N, h=h: lambda eh: eh.activation(
                    out=probs[jp][:, 0:N], in_=psc[jp][:, 0:N], func=AF.Exp,
                    bias=negc[:, gmaj(h):gmaj(h) + 1], scale=SM_SCALE))(),
                    reads=[("psc", jp), "negc"], writes=[("probs", jp)])
                s_idx = v // 8
                P.op("ve", (lambda jp=jp, s_idx=s_idx: lambda eh: eh.tensor_tensor(
                    out=probs[jp][:, 0:128], in0=probs[jp][:, 0:128],
                    in1=mask_sb[:, s_idx * 128:(s_idx + 1) * 128], op=ALU.mult))(),
                    reads=[("probs", jp), "mask"], writes=[("probs", jp)])
                for i in range(i0, 8):
                    bank, off = PV_SLOT[i]
                    st = (v == 0) and (i in (0, 3, 6))
                    P.op("te", (lambda jp=jp, j=j, v=v, i=i, i0=i0, bank=bank, off=off, st=st:
                                lambda eh: eh.matmul(
                        out=ppv[bank][:, off:off + 130],
                        lhsT=probs[jp][:, (i - i0) * 128:(i - i0 + 1) * 128],
                        rhs=Vb[j][:, v, 0:130],
                        start=st, stop=(v == 24 + i), skip_group_check=True))(),
                        reads=[("probs", jp), ("Vb", j)], writes=[("ppv", bank)])
            for i in range(NQT):
                bank, off = PV_SLOT[i]
                P.op("ve", (lambda bank=bank, off=off, i=i: lambda eh: eh.reciprocal(
                    out=rinv[:, i:i + 1], in_=ppv[bank][:, off + 128:off + 129]))(),
                    reads=[("ppv", bank)], writes=["rinv"])
                P.op("ve", (lambda bank=bank, off=off, i=i, h=h: lambda eh: eh.tensor_scalar_mul(
                    out=attn_tm[i][:, h * 128:(h + 1) * 128],
                    in0=ppv[bank][:, off:off + 128], scalar1=rinv[:, i:i + 1]))(),
                    reads=[("ppv", bank), "rinv"], writes=[("atm", i)])
            for i in range(NQT):
                P.op("se", (lambda i=i, h=h: lambda eh: eh.dma_start(
                    out=attn_d[i * 128:(i + 1) * 128, h * 128:(h + 1) * 128],
                    in_=attn_tm[i][:, h * 128:(h + 1) * 128]))(),
                    reads=[("atm", i)], writes=[("attn_d", h)], dma=True)
    ps2.close()
    p2.close()
    P.fence()

    p3 = ExitStack()
    SB3 = lambda name, shape, dt: p3.enter_context(nc.sbuf_tensor(name, shape, dt))
    xsfm = [SB3(f"xsfm{k}", [128, TOK], BF16) for k in range(32)]
    wo_st = SB3("wo_st", [128, 32, NB], BF16)
    wt_st = SB3("wt_st", [128, 32, NB], BF16)
    wo_f = [SB3(f"wo_f{j}", [128, 512], F32) for j in range(2)]
    s_sb = [SB3(f"s_sb{m}", [128, NB], BF16) for m in range(8)]
    bias_bc = SB3("bias_bc", [128, H], BF16)
    bias_b = SB3("bias_b", [1, H], BF16)
    ocs_b = SB3("ocs_b", [8, H], BF16)
    ys_sb = [SB3(f"ys_sb{j}", [128, NB], F32) for j in range(2)]
    t1_sb = [SB3(f"t1_sb{j}", [128, NB], F32) for j in range(2)]
    g_sb = [SB3(f"g_sb{j}", [128, NB], BF16) for j in range(2)]
    h_sb = [SB3(f"h_sb{j}", [128, NB], BF16) for j in range(2)]
    xe = [SB3(f"xe{j}", [128, NB], F32) for j in range(2)]
    oe = [SB3(f"oe{j}", [128, NB], F32) for j in range(2)]
    ps3 = ExitStack()
    pp = [ps3.enter_context(nc.psum_tensor(f"pp{i}", [128, 512], F32)) for i in range(6)]
    pos = ps3.enter_context(nc.psum_tensor("pos", [128, 512], F32))
    pbb = ps3.enter_context(nc.psum_tensor("pbb", [128, 512], F32))

    for k in range(32):
        P.op("sq", (lambda k=k: lambda eh: eh.dma_start(
            out=Qfm[k][:], in_=attn_d[0:TOK, k * 128:(k + 1) * 128], transpose=True))(),
            reads=[("attn_d", k)], writes=[("Q", k)], dma=True)
    for k in range(32):
        P.op("sq", (lambda k=k: lambda eh: eh.dma_start(
            out=xsfm[k][:], in_=xs_d[k * 128:(k + 1) * 128, :]))(),
            reads=["xs_d"], writes=[("xs", k)], dma=True)
    P.op("gq", dma("gq", bias_b[:], bias1[:]), writes=["biasb"], dma=True)
    for n8 in range(8):
        P.op("te", (lambda n8=n8: lambda eh: eh.matmul(
            out=pbb[:, 0:512], lhsT=ones_row[:], rhs=bias_b[0:1, n8 * 512:(n8 + 1) * 512],
            start=True, stop=True))(),
            reads=["biasb", "ones_row"], writes=["pbb"])
        P.op("ve", (lambda n8=n8: lambda eh: eh.tensor_copy(
            out=bias_bc[:, n8 * 512:(n8 + 1) * 512], in_=pbb[:, 0:512]))(),
            reads=["pbb"], writes=["biasbc"])
    P.op("gq", dma("gq", ocs_b[:], outcs[:]), writes=["ocsb"], dma=True)

    for n in range(NNB):
        for k2 in range(16):
            j = k2 % 2
            r0 = k2 * 256
            P.op("sq", (lambda j=j, r0=r0, n=n: lambda eh: eh.dma_start(
                out=wo_f[j][0:128, 0:256], in_=wo[r0:r0 + 128, n * NB:(n + 1) * NB]))(),
                writes=[("wo_f", j)], dma=True)
            P.op("sq", (lambda j=j, r0=r0, n=n: lambda eh: eh.dma_start(
                out=wo_f[j][0:128, 256:512], in_=wo[r0 + 128:r0 + 256, n * NB:(n + 1) * NB]))(),
                writes=[("wo_f", j)], dma=True)
            P.op("se", (lambda j=j, k2=k2: lambda eh: eh.copy(
                out=wo_st[:, 2 * k2, :], in_=wo_f[j][0:128, 0:256]))(),
                reads=[("wo_f", j)], writes=["wo_st"])
            P.op("se", (lambda j=j, k2=k2: lambda eh: eh.copy(
                out=wo_st[:, 2 * k2 + 1, :], in_=wo_f[j][0:128, 256:512]))(),
                reads=[("wo_f", j)], writes=["wo_st"])
        for m in range(8):
            slot = m % 6
            for k in range(32):
                P.op("te", (lambda slot=slot, m=m, k=k: lambda eh: eh.matmul(
                    out=pp[slot][:, 0:NB], lhsT=Qfm[k][:, m * 128:(m + 1) * 128],
                    rhs=wo_st[:, k, :], start=(k == 0), stop=(k == 31)))(),
                    reads=[("Q", k), "wo_st"], writes=[("pp", slot)])
            P.op("ve", (lambda slot=slot, m=m: lambda eh: eh.tensor_copy(
                out=s_sb[m][:], in_=pp[slot][:, 0:NB]))(),
                reads=[("pp", slot)], writes=[("s", m)])
        for k in range(32):
            P.op("sq", (lambda k=k, n=n: lambda eh: eh.dma_start(
                out=wt_st[:, k, :], in_=wTs_d[k * 128:(k + 1) * 128, n * NB:(n + 1) * NB]))(),
                reads=["wTs"], writes=["wt_st"], dma=True)
        for m in range(8):
            slot = m % 6
            for k in range(32):
                P.op("te", (lambda slot=slot, m=m, k=k: lambda eh: eh.matmul(
                    out=pp[slot][:, 0:NB], lhsT=xsfm[k][:, m * 128:(m + 1) * 128],
                    rhs=wt_st[:, k, :], start=(k == 0), stop=(k == 31)))(),
                    reads=[("xs", k), "wt_st"], writes=[("pp", slot)])
            j = m % 2
            P.op("ve", (lambda slot=slot, j=j: lambda eh: eh.tensor_copy(
                out=ys_sb[j][:], in_=pp[slot][:, 0:NB]))(),
                reads=[("pp", slot)], writes=[("ys", j)])
            P.op("te", (lambda m=m, n=n: lambda eh: eh.matmul(
                out=pos[:, 0:NB], lhsT=routT[0:8, m * 128:(m + 1) * 128],
                rhs=ocs_b[0:8, n * NB:(n + 1) * NB], start=True, stop=True))(),
                reads=["routT", "ocsb"], writes=["pos"])
            P.op("ve", (lambda j=j: lambda eh: eh.tensor_tensor(
                out=t1_sb[j][:], in0=ys_sb[j][:], in1=pos[:, 0:NB], op=ALU.mult))(),
                reads=[("ys", j), "pos"], writes=[("t1", j)])
            P.op("ve", (lambda j=j, n=n: lambda eh: eh.tensor_tensor(
                out=t1_sb[j][:], in0=t1_sb[j][:], in1=bias_bc[:, n * NB:(n + 1) * NB],
                op=ALU.add))(),
                reads=[("t1", j), "biasbc"], writes=[("t1", j)])
            P.op("se", (lambda j=j: lambda eh: eh.activation(
                out=g_sb[j][:], in_=t1_sb[j][:], func=AF.Silu))(),
                reads=[("t1", j)], writes=[("g", j)])
            P.op("ve", (lambda j=j, m=m: lambda eh: eh.tensor_tensor(
                out=h_sb[j][:], in0=g_sb[j][:], in1=s_sb[m][:], op=ALU.mult))(),
                reads=[("g", j), ("s", m)], writes=[("h", j)])
            P.op("se", (lambda j=j, m=m, n=n: lambda eh: eh.dma_start(
                out=xe[j][:], in_=x_own[m * 128:(m + 1) * 128, n * NB:(n + 1) * NB]))(),
                writes=[("xe", j)], dma=True)
            P.op("ve", (lambda j=j: lambda eh: eh.tensor_tensor(
                out=oe[j][:], in0=h_sb[j][:], in1=xe[j][:], op=ALU.add))(),
                reads=[("h", j), ("xe", j)], writes=[("oe", j)])
            P.op("se", (lambda j=j, m=m, n=n: lambda eh: eh.dma_start(
                out=out_d[m * 128:(m + 1) * 128, n * NB:(n + 1) * NB], in_=oe[j][:]))(),
                reads=[("oe", j)], writes=["out_d"], dma=True)
    ps3.close()
    p3.close()

    P.emit(nc, es, regs_init, final_waits_eng="se")
    return nc


def kernel(L0_attn_hidden_states, L0_attn_w_q, L0_attn_w_k, L0_attn_w_v, L0_attn_w_o,
           L0_ffn_weight, L0_ffn_bias, L0_ffn_gate_w, L0_ffn_in_channel_scale,
           L0_ffn_out_channel_scale):
    hs = np.ascontiguousarray(np.asarray(L0_attn_hidden_states, dtype=np.float32))
    wq = np.ascontiguousarray(np.asarray(L0_attn_w_q, dtype=np.float32))
    wk = np.ascontiguousarray(np.asarray(L0_attn_w_k, dtype=np.float32))
    wv = np.ascontiguousarray(np.asarray(L0_attn_w_v, dtype=np.float32))
    wo = np.ascontiguousarray(np.asarray(L0_attn_w_o, dtype=np.float32))
    wT = np.ascontiguousarray(np.asarray(L0_ffn_weight, dtype=np.float32).T)
    gwT = np.ascontiguousarray(np.asarray(L0_ffn_gate_w, dtype=np.float32).T)
    bias1 = np.ascontiguousarray(np.asarray(L0_ffn_bias, dtype=np.float32).reshape(1, H))
    incs = np.ascontiguousarray(np.asarray(L0_ffn_in_channel_scale, dtype=np.float32))
    outcs = np.ascontiguousarray(np.asarray(L0_ffn_out_channel_scale, dtype=np.float32))

    hs_t = hs.reshape(B, 8, 4, 128, H)
    ident = np.eye(128, dtype=np.float32).astype(ml_dtypes.bfloat16)
    nv = (int.from_bytes(os.urandom(4), 'little') & 0x7FFFFFFF) | 1
    nonce = np.full((1, 4), nv, dtype=np.int32)

    kk = np.arange(128)[:, None]
    qq = np.arange(128)[None, :]
    in_maps = []
    for c in range(8):
        b, r = c // 4, c % 4
        x_own_a = np.ascontiguousarray(hs_t[b, :, r].reshape(TOK, H))
        x_far_a = np.ascontiguousarray(hs_t[b, :, (r + 2) % 4].reshape(TOK, H))
        wm = np.zeros((128, 512), dtype=np.float32)
        for s in range(4):
            allowed = (kk - qq) <= (r - s) * 128
            wm[:, s * 128:(s + 1) * 128] = allowed.astype(np.float32)
        in_maps.append({
            "x_own": x_own_a, "x_far": x_far_a, "wq": wq, "wk": wk, "wv": wv, "wo": wo,
            "wT": wT, "gwT": gwT, "bias1": bias1, "incs": incs, "outcs": outcs,
            "wmask": wm.astype(ml_dtypes.bfloat16), "ident_in": ident,
            "nonce_in": nonce,
        })

    nc = build_program()
    res = run_bass_kernel_spmd(nc, in_maps, core_ids=list(range(8)))
    out = np.empty((B, S, H), dtype=np.float32)
    out_t = out.reshape(B, 8, 4, 128, H)
    for c in range(8):
        b, r = c // 4, c % 4
        out_t[b, :, r] = res.results[c]["out_d"].reshape(8, 128, H)
    return out


# version 18
# speedup vs baseline: 1.2983x; 1.2983x over previous
import math
import os
import numpy as np
import ml_dtypes
from contextlib import ExitStack

import concourse.bass as bass
import concourse.mybir as mybir
from concourse.bass_utils import run_bass_kernel_spmd

F32 = mybir.dt.float32
BF16 = mybir.dt.bfloat16
I32 = mybir.dt.int32
U32 = mybir.dt.uint32
AF = mybir.ActivationFunctionType
ALU = mybir.AluOpType
AX = mybir.AxisListType

B, S, H = 2, 4096, 4096
HKV, GQ, D = 8, 4, 128
DKV = HKV * D
NQT = 8
TOK = NQT * 128
SM_SCALE = 1.0 / math.sqrt(D)
NB = 256
NNB = H // NB
EPS = 1e-5

ENGS = ("te", "ve", "se", "gq", "sq")


def gmaj(h):
    return (h % GQ) * HKV + h // GQ


class Prog:

    def __init__(self):
        self.ops = []

    def op(self, eng, build, reads=(), writes=(), dma=False):
        assert eng in ENGS
        self.ops.append((eng, build, tuple(reads), tuple(writes), dma))

    def fence(self):
        self.ops.append(("FENCE", None, (), (), False))

    def emit(self, nc, es, regs_init, final_waits_eng="se"):
        cs = {e: es.enter_context(nc.semaphore(f"cs_{e}")) for e in ENGS}
        ds = {e: es.enter_context(nc.semaphore(f"ds_{e}")) for e in ("gq", "sq", "se")}
        cnt_c = {e: 0 for e in ENGS}
        cnt_d = {e: 0 for e in ("gq", "sq", "se")}
        plan = []
        last_w = {}
        readers = {}
        fence_low = {}
        expanded = []
        for entry in self.ops:
            if entry[0] == "FENCE":
                expanded.append(entry)
            else:
                expanded.append(entry)
        self.ops = expanded
        for (eng, build, reads, writes, dma) in self.ops:
            if eng == "FENCE":
                plan.append(("FENCE", dict((("c", e), cnt_c[e]) for e in ENGS)
                             | dict((("d", e), cnt_d[e]) for e in ("gq", "sq", "se")), None))
                continue
            if dma:
                cnt_d[eng] += 16
                mykey, myval = ("d", eng), cnt_d[eng]
            else:
                cnt_c[eng] += 1
                mykey, myval = ("c", eng), cnt_c[eng]
            need = {}
            PSUM_KEYS = ("pj", "pd", "peb", "psc", "ppv", "pp", "pos", "pbb")

            def is_psum(b):
                t = b[0] if isinstance(b, tuple) else b
                return t in PSUM_KEYS

            def add(dep, buf=None):
                if dep is None:
                    return
                k, v = dep
                if k == mykey and v == myval:
                    return
                if (not dma) and k == mykey and eng == "te" and buf is not None \
                        and is_psum(buf):
                    return
                if need.get(k, -1) < v:
                    need[k] = v
            for b in reads:
                add(last_w.get(b), b)
            for b in writes:
                add(last_w.get(b), b)
                for dep in readers.get(b, ()):
                    add(dep, None if not is_psum(b) else b)
            for b in writes:
                last_w[b] = (mykey, myval)
                readers[b] = []
            for b in reads:
                readers.setdefault(b, []).append((mykey, myval))
            plan.append((mykey, myval, need))

        sem_of = {("c", e): cs[e] for e in ENGS}
        for e in ("gq", "sq", "se"):
            sem_of[("d", e)] = ds[e]
        by_eng = {e: [i for i, o in enumerate(self.ops) if o[0] in (e, "FENCE")] for e in ENGS}
        ops = self.ops
        blk = es.enter_context(nc.Block())
        satisfied = {e: {} for e in ENGS}

        def make_body(eng_name):
            idxs = by_eng[eng_name]

            def body(eh):
                if eng_name == "gq" and regs_init is not None:
                    regs_init(eh)
                sat = satisfied[eng_name]
                for i in idxs:
                    if ops[i][0] == "FENCE":
                        totals = plan[i][1]
                        for k in sorted(totals, key=repr):
                            v = totals[k]
                            if v <= 0 or sat.get(k, 0) >= v:
                                continue
                            eh.wait_ge(sem_of[k], v)
                            sat[k] = v
                        continue
                    _, build, _, _, dma = ops[i]
                    mykey, myval, need = plan[i]
                    for k in sorted(need, key=repr):
                        v = need[k]
                        if sat.get(k, 0) >= v:
                            continue
                        eh.wait_ge(sem_of[k], v)
                        sat[k] = v
                    inst = build(eh)
                    if inst is None:
                        inst = eh.nop()
                    inst.then_inc(sem_of[mykey], 16 if dma else 1)
                if eng_name == final_waits_eng:
                    for e in ("se", "sq", "gq"):
                        if cnt_d[e] > 0:
                            eh.wait_ge(ds[e], cnt_d[e])

            return body

        blk.tensor(make_body("te"))
        blk.vector(make_body("ve"))
        blk.scalar(make_body("se"))
        blk.gpsimd(make_body("gq"))
        blk.sync(make_body("sq"))


def build_program():
    nc = bass.Bass("TRN2", target_bir_lowering=False)

    x_own = nc.dram_tensor("x_own", [TOK, H], F32, kind="ExternalInput")
    x_far = nc.dram_tensor("x_far", [TOK, H], F32, kind="ExternalInput")
    wq = nc.dram_tensor("wq", [H, H], F32, kind="ExternalInput")
    wk = nc.dram_tensor("wk", [H, DKV], F32, kind="ExternalInput")
    wv = nc.dram_tensor("wv", [H, DKV], F32, kind="ExternalInput")
    wo = nc.dram_tensor("wo", [H, H], F32, kind="ExternalInput")
    wT = nc.dram_tensor("wT", [H, H], F32, kind="ExternalInput")
    gwT = nc.dram_tensor("gwT", [H, 8], F32, kind="ExternalInput")
    bias1 = nc.dram_tensor("bias1", [1, H], F32, kind="ExternalInput")
    incs = nc.dram_tensor("incs", [8, H], F32, kind="ExternalInput")
    outcs = nc.dram_tensor("outcs", [8, H], F32, kind="ExternalInput")
    wmask = nc.dram_tensor("wmask", [128, 512], BF16, kind="ExternalInput")
    ident_in = nc.dram_tensor("ident_in", [128, 128], BF16, kind="ExternalInput")
    nonce_in = nc.dram_tensor("nonce_in", [1, 4], I32, kind="ExternalInput")
    out_d = nc.dram_tensor("out_d", [TOK, H], F32, kind="ExternalOutput")

    xn_own_d = nc.dram_tensor("xn_own_d", [TOK, H], BF16)
    xn_far_d = nc.dram_tensor("xn_far_d", [TOK, H], BF16)
    xs_d = nc.dram_tensor("xs_d", [H, TOK], BF16)
    attn_d = nc.dram_tensor("attn_d", [TOK, H], BF16)
    wTs_d = nc.dram_tensor("wTs_d", [H, H], BF16)
    kvK = nc.dram_tensor("kvK", [S, 1024], BF16, addr_space="Shared")
    kvK_st = nc.dram_tensor("kvK_st", [1024, 1024], BF16)
    kvV_st = nc.dram_tensor("kvV_st", [1024, 1024], BF16)
    kvV = nc.dram_tensor("kvV", [S, DKV], BF16, addr_space="Shared")
    meta_n = nc.dram_tensor("meta_n", [2, 512], F32, addr_space="Shared")
    meta_f = nc.dram_tensor("meta_f", [2, 128], I32, addr_space="Shared")

    P = Prog()
    es = ExitStack()
    SB = lambda name, shape, dt: es.enter_context(nc.sbuf_tensor(name, shape, dt))

    Qfm = [SB(f"Qfm{m}", [128, TOK], BF16) for m in range(32)]
    mask_sb = SB("mask_sb", [128, 512], BF16)
    id_sb = SB("id_sb", [128, 128], BF16)
    ones_col = SB("ones_col", [128, 1], BF16)
    ones_row = SB("ones_row", [1, 128], BF16)
    eps_sb = SB("eps_sb", [128, 1], F32)
    lnsc_sb = SB("lnsc_sb", [1, 1], F32)
    routT = SB("routT", [8, TOK], BF16)
    qn2 = SB("qn2", [1, 32], F32)
    kn_mine = SB("kn_mine", [1, 8], F32)
    kn_par = SB("kn_par", [1, 8], F32)
    nred = SB("nred", [1, 2], F32)
    pid_sb = SB("pid_sb", [1, 1], U32)
    nonce_sb = SB("nonce_sb", [1, 4], I32)
    flag_sb = SB("flag_sb", [1, 4], I32)

    def dma(q, out, in_, **kw):
        return lambda eh: eh.dma_start(out=out, in_=in_, **kw)

    P.op("gq", dma("gq", pid_sb[:], nc.partition_id_tensor[:]), writes=["pid"], dma=True)
    P.op("gq", dma("gq", nonce_sb[:], nonce_in[:]), writes=["nonce"], dma=True)
    P.op("sq", dma("sq", mask_sb[:], wmask[:]), writes=["mask"], dma=True)
    P.op("sq", dma("sq", id_sb[:], ident_in[:]), writes=["ident"], dma=True)
    P.op("ve", lambda eh: eh.memset(ones_col[:], 1.0), writes=["ones_col"])
    P.op("ve", lambda eh: eh.memset(ones_row[:], 1.0), writes=["ones_row"])
    P.op("ve", lambda eh: eh.memset(eps_sb[:], EPS), writes=["eps"])
    P.op("ve", lambda eh: eh.memset(lnsc_sb[:], math.log(SM_SCALE)), writes=["lnsc"])

    REG = {}

    def regs_init(g):
        names = ["pid", "r", "p", "s2", "vr0", "vr1",
                 "mn_own", "mn_par", "fl_own", "fl_par", "tmp",
                 "tgt", "nd", "fv", "nonce"]
        for n in names:
            REG[n] = es.enter_context(g.register(f"rg_{n}"))

    def regs_compute(g):
        R = REG
        g.reg_load(R["pid"], pid_sb[:1, :1])
        g.reg_load(R["nonce"], nonce_sb[:1, :1])
        g.reg_mod(R["r"], R["pid"], 4)
        g.reg_mod(R["p"], R["pid"], 2)
        g.reg_add(R["s2"], R["r"], 2)
        g.reg_mod(R["s2"], R["s2"], 4)
        g.reg_mov(R["vr0"], 0); g.reg_add(R["vr0"], R["vr0"], R["r"]); g.reg_mul(R["vr0"], R["vr0"], 1024 * 1024)
        g.reg_mov(R["vr1"], 0); g.reg_add(R["vr1"], R["vr1"], R["s2"]); g.reg_mul(R["vr1"], R["vr1"], 1024 * 1024)
        g.reg_mov(R["mn_own"], 0); g.reg_add(R["mn_own"], R["mn_own"], R["p"]); g.reg_mul(R["mn_own"], R["mn_own"], 512)
        g.reg_mov(R["mn_par"], 512); g.reg_sub(R["mn_par"], R["mn_par"], R["mn_own"])
        g.reg_mov(R["fl_own"], 0); g.reg_add(R["fl_own"], R["fl_own"], R["p"]); g.reg_mul(R["fl_own"], R["fl_own"], 128)
        g.reg_mov(R["fl_par"], 128); g.reg_sub(R["fl_par"], R["fl_par"], R["fl_own"])
        return None

    P.op("gq", regs_compute, reads=["pid", "nonce"], writes=["regs"])

    p0 = ExitStack()
    SB0 = lambda name, shape, dt: p0.enter_context(nc.sbuf_tensor(name, shape, dt))
    xt = [SB0(f"p0_x{j}", [128, H], F32) for j in range(2)]
    xnt = [SB0(f"p0_xn{j}", [128, H], BF16) for j in range(2)]
    st6 = [SB0(f"p0_st6{j}", [128, 48], F32) for j in range(2)]
    mv = [SB0(f"p0_mv{j}", [128, 2], F32) for j in range(2)]
    rstd = [SB0(f"p0_rs{j}", [128, 2], F32) for j in range(2)]
    wts_f = [SB0(f"p0_wtf{j}", [128, 2048], F32) for j in range(2)]
    wts_b = [SB0(f"p0_wtb{j}", [128, 2048], BF16) for j in range(2)]

    for t in range(16):
        src = x_own if t < 8 else x_far
        dstd = xn_own_d if t < 8 else xn_far_d
        dtag = "xnod" if t < 8 else "xnfd"
        row = (t % 8) * 128
        j = t % 2
        P.op("se", dma("se", xt[j][:], src[row:row + 128, :]), writes=[("xt", j)], dma=True)
        for i in range(8):
            P.op("ve", (lambda j=j, i=i: lambda eh: eh.bn_stats(
                out=st6[j][:, i * 6:(i + 1) * 6], in_=xt[j][:, i * 512:(i + 1) * 512]))(),
                reads=[("xt", j)], writes=[("st6", j)])
        P.op("ve", (lambda j=j: lambda eh: eh.bn_aggr(out=mv[j][:], in_=st6[j][:]))(),
             reads=[("st6", j)], writes=[("mv", j)])
        P.op("se", (lambda j=j: lambda eh: eh.activation(
            out=rstd[j][:, 0:1], in_=mv[j][:, 1:2], func=AF.Ln, bias=eps_sb[:, 0:1]))(),
            reads=[("mv", j), "eps"], writes=[("rstd", j)])
        P.op("se", (lambda j=j: lambda eh: eh.activation(
            out=rstd[j][:, 1:2], in_=rstd[j][:, 0:1], func=AF.Exp, scale=-0.5))(),
            reads=[("rstd", j)], writes=[("rstd", j)])
        P.op("ve", (lambda j=j: lambda eh: eh.tensor_scalar(
            out=xnt[j][:], in0=xt[j][:], scalar1=mv[j][:, 0:1], scalar2=rstd[j][:, 1:2],
            op0=ALU.subtract, op1=ALU.mult))(),
            reads=[("xt", j), ("mv", j), ("rstd", j)], writes=[("xnt", j)])
        P.op("se", dma("se", dstd[row:row + 128, :], xnt[j][:]),
             reads=[("xnt", j)], writes=[dtag], dma=True)

    for u in range(64):
        j = u % 2
        r0 = (u // 2) * 128
        c0 = (u % 2) * 2048
        P.op("sq", dma("sq", wts_f[j][:], wT[r0:r0 + 128, c0:c0 + 2048]),
             writes=[("wtsf", j)], dma=True)
        P.op("se", (lambda j=j: lambda eh: eh.sign(out=wts_b[j][:], in_=wts_f[j][:]))(),
             reads=[("wtsf", j)], writes=[("wtsb", j)])
        P.op("sq", dma("sq", wTs_d[r0:r0 + 128, c0:c0 + 2048], wts_b[j][:]),
             reads=[("wtsb", j)], writes=["wTs"], dma=True)
    p0.close()
    P.fence()

    p1 = ExitStack()
    SB1 = lambda name, shape, dt: p1.enter_context(nc.sbuf_tensor(name, shape, dt))
    xnfm = [SB1(f"xnfm{k}", [128, TOK], BF16) for k in range(32)]
    ps1 = ExitStack()
    pj = [ps1.enter_context(nc.psum_tensor(f"pj{i}", [128, 1024], F32)) for i in range(3)]
    pd = ps1.enter_context(nc.psum_tensor("pd", [128, 512], F32))
    peb = ps1.enter_context(nc.psum_tensor("peb", [128, 256], BF16))

    def xn_load(which):
        srcd = xn_own_d if which == "own" else xn_far_d
        srctag = "xnod" if which == "own" else "xnfd"
        for k in range(32):
            P.op("sq", dma("sq", xnfm[k][:], srcd[0:TOK, k * 128:(k + 1) * 128], transpose=True),
                 reads=[srctag], writes=[("xnfm", k)], dma=True)

    xn_load("own")

    WS = {}

    def open_wscope():
        st = ExitStack()
        WS["stack"] = st
        WS["wcol_f"] = [st.enter_context(nc.sbuf_tensor(f"wcf{next(_ctr)}", [128, 32, 128], F32)) for j in range(2)]
        WS["wcol_b"] = [st.enter_context(nc.sbuf_tensor(f"wcb{next(_ctr)}", [128, 32, 128], BF16)) for j in range(2)]
        WS["sqt"] = st.enter_context(nc.sbuf_tensor(f"sqt{next(_ctr)}", [128, 1024], BF16))
        WS["kevac"] = [st.enter_context(nc.sbuf_tensor(f"kev{next(_ctr)}", [128, 1024], BF16)) for j in range(2)]
        return st

    import itertools
    _ctr = itertools.count()

    def load_wcol(wten, ncols, m, j):
        wcol_f, wcol_b = WS["wcol_f"], WS["wcol_b"]
        for piece in range(4):
            k0 = piece * 8
            src = bass.AP(wten, (k0 * 128) * ncols + m * 128,
                          [[ncols, 128], [128 * ncols, 8], [1, 128]])
            P.op("sq", dma("sq", wcol_f[j][:, k0:k0 + 8, :], src),
                 writes=[("wcf", j)], dma=True)
        wb = wcol_b[j]
        P.op("se", (lambda j=j, wb=wb: lambda eh: eh.copy(out=wb[:], in_=WS["wcol_f"][j][:] if False else wb[:]))() if False else
             (lambda j=j: lambda eh: eh.copy(out=WS["wcol_b"][j][:], in_=WS["wcol_f"][j][:]))(),
             reads=[("wcf", j)], writes=[("wcb", j)])

    def col_norm_max(src_bf, srckey, dst_ap, dstkey):
        sqt = WS["sqt"]
        P.op("se", (lambda sqt=sqt, src_bf=src_bf: lambda eh: eh.activation(
            out=sqt[:], in_=src_bf[:], func=AF.Square))(),
            reads=[srckey], writes=["sqt"])
        for half in range(2):
            P.op("te", (lambda half=half, sqt=sqt: lambda eh: eh.matmul(
                out=pd[0:1, 0:512], lhsT=ones_col[:], rhs=sqt[:, half * 512:(half + 1) * 512],
                start=True, stop=True))(),
                reads=["sqt", "ones_col"], writes=["pd"])
            P.op("ve", (lambda half=half: lambda eh: eh.reduce_max(
                out=nred[0:1, half:half + 1], in_=pd[0:1, 0:512], axis=AX.X))(),
                reads=["pd"], writes=["nred"])
        P.op("ve", lambda eh: eh.tensor_max(out=nred[0:1, 0:1], in0=nred[0:1, 0:1],
                                            in1=nred[0:1, 1:2]),
             reads=["nred"], writes=["nred"])
        P.op("ve", (lambda dst_ap=dst_ap: lambda eh: eh.tensor_max(
            out=dst_ap, in0=dst_ap, in1=nred[0:1, 0:1]))(),
            reads=["nred", dstkey], writes=[dstkey])

    P.op("ve", lambda eh: eh.memset(qn2[:], 0.0), writes=["qn2"])
    P.op("ve", lambda eh: eh.memset(kn_mine[:], 0.0), writes=["knm"])

    def q_proj():
        for m in range(32):
            j = m % 2
            load_wcol(wq, H, m, j)
            slot = m % 3
            for k in range(32):
                for half in range(2):
                    P.op("te", (lambda slot=slot, j=j, k=k, half=half: lambda eh: eh.matmul(
                        out=pj[slot][:, half * 512:(half + 1) * 512],
                        lhsT=WS["wcol_b"][j][:, k, :], rhs=xnfm[k][:, half * 512:(half + 1) * 512],
                        start=(k == 0), stop=(k == 31)))(),
                        reads=[("wcb", j), ("xnfm", k)], writes=[("pj", slot)])
            P.op("ve", (lambda m=m, slot=slot: lambda eh: eh.tensor_copy(
                out=Qfm[m][:], in_=pj[slot][:]))(),
                reads=[("pj", slot)], writes=[("Q", m)])
            col_norm_max(Qfm[m], ("Q", m), qn2[0:1, gmaj(m):gmaj(m) + 1], "qn2")

    def k_proj(pass_far):
        creg = "vr1" if pass_far else "vr0"
        sttag = "kvKst_f" if pass_far else "kvKst_o"
        for hk in range(HKV):
            j = hk % 2
            load_wcol(wk, DKV, hk, j)
            slot = hk % 3
            for k in range(32):
                for half in range(2):
                    P.op("te", (lambda slot=slot, j=j, k=k, half=half: lambda eh: eh.matmul(
                        out=pj[slot][:, half * 512:(half + 1) * 512],
                        lhsT=WS["wcol_b"][j][:, k, :], rhs=xnfm[k][:, half * 512:(half + 1) * 512],
                        start=(k == 0), stop=(k == 31)))(),
                        reads=[("wcb", j), ("xnfm", k)], writes=[("pj", slot)])
            je = hk % 2
            kv_t = WS["kevac"][je]
            P.op("ve", (lambda slot=slot, kv_t=kv_t: lambda eh: eh.tensor_copy(
                out=kv_t[:], in_=pj[slot][:]))(),
                reads=[("pj", slot)], writes=[("kev", je)])
            col_norm_max(kv_t, ("kev", je), kn_mine[0:1, hk:hk + 1], "knm")

            P.op("sq", (lambda hk=hk, kv_t=kv_t: lambda eh: eh.dma_start(
                out=kvK_st[hk * 128:(hk + 1) * 128, :], in_=kv_t[:]))(),
                reads=[("kev", je), "kvK_cp"], writes=["kvKst"], dma=True)

        def k_copy(eh, creg=creg):
            g = eh
            dst = bass.AP(kvK, REG[creg], [[1024, 1024], [1, 1024]])
            return g.dma_start(out=dst, in_=kvK_st[:])
        P.op("gq", k_copy, reads=["kvKst", "regs"], writes=["kvK_w", "kvK_cp"], dma=True)

    def v_proj(pass_far):
        vreg = "vr1" if pass_far else "vr0"
        p1b = ExitStack()
        tag = "f" if pass_far else "o"
        wvf = [p1b.enter_context(nc.sbuf_tensor(f"wvf{tag}{j}", [128, 512], F32)) for j in range(2)]
        wvb = p1b.enter_context(nc.sbuf_tensor(f"wvb{tag}", [128, 32, 512], BF16))
        vev8 = [p1b.enter_context(nc.sbuf_tensor(f"vev{tag}{mt}", [128, 1024], BF16)) for mt in range(8)]
        for half in range(2):
            for k in range(32):
                j = k % 2
                P.op("sq", (lambda j=j, k=k, half=half: lambda eh: eh.dma_start(
                    out=wvf[j][:], in_=wv[k * 128:(k + 1) * 128, half * 512:(half + 1) * 512]))(),
                    writes=[("wvf", j)], dma=True)
                P.op("se", (lambda j=j, k=k: lambda eh: eh.copy(out=wvb[:, k, :], in_=wvf[j][:]))(),
                     reads=[("wvf", j)], writes=["wvb"])
            for mt in range(8):
                slot = mt % 3
                for k in range(32):
                    P.op("te", (lambda slot=slot, k=k, mt=mt: lambda eh: eh.matmul(
                        out=pj[slot][:, 0:512],
                        lhsT=xnfm[k][:, mt * 128:(mt + 1) * 128],
                        rhs=wvb[:, k, :],
                        start=(k == 0), stop=(k == 31)))(),
                        reads=["wvb", ("xnfm", k)], writes=[("pj", slot)])
                P.op("ve", (lambda slot=slot, mt=mt, half=half: lambda eh: eh.tensor_copy(
                    out=vev8[mt][:, half * 512:(half + 1) * 512], in_=pj[slot][:, 0:512]))(),
                    reads=[("pj", slot)], writes=[("vev", mt)])
        for mt in range(8):
            P.op("sq", (lambda mt=mt, vt=vev8[mt]: lambda eh: eh.dma_start(
                out=kvV_st[mt * 128:(mt + 1) * 128, :], in_=vt[:]))(),
                reads=[("vev", mt), "kvV_cp"], writes=["kvVst"], dma=True)

        def v_copy(eh, vreg=vreg):
            g = eh
            dst = bass.AP(kvV, REG[vreg], [[1024, 1024], [1, 1024]])
            return g.dma_start(out=dst, in_=kvV_st[:])
        P.op("gq", v_copy, reads=["kvVst", "regs"], writes=["kvV_w", "kvV_cp"], dma=True)
        p1b.close()
        P.fence()

    ws = open_wscope()
    q_proj()
    k_proj(False)
    ws.close()
    P.fence()
    v_proj(False)

    p1c = ExitStack()
    gwb = p1c.enter_context(nc.sbuf_tensor("gwb", [128, 32, 8], BF16))
    rtmp = p1c.enter_context(nc.sbuf_tensor("rtmp", [128, 12], F32))
    rbf = p1c.enter_context(nc.sbuf_tensor("rbf", [128, 8], BF16))
    gsrc = bass.AP(gwT, 0, [[8, 128], [1024, 32], [1, 8]])
    P.op("gq", dma("gq", gwb[:], gsrc, allow_slow_non_contiguous=True), writes=["gwb"], dma=True)
    for m in range(8):
        for k in range(32):
            P.op("te", (lambda m=m, k=k: lambda eh: eh.matmul(
                out=pd[:, 0:8], lhsT=xnfm[k][:, m * 128:(m + 1) * 128], rhs=gwb[:, k, :],
                start=(k == 0), stop=(k == 31)))(),
                reads=["gwb", ("xnfm", k)], writes=["pd"])
        P.op("ve", lambda eh: eh.reduce_max(out=rtmp[:, 8:9], in_=pd[:, 0:8], axis=AX.X,
                                            negate=True),
             reads=["pd"], writes=["rtmp"])
        P.op("se", lambda eh: eh.activation(out=rtmp[:, 0:8], in_=pd[:, 0:8], func=AF.Exp,
                                            bias=rtmp[:, 8:9]),
             reads=["pd", "rtmp"], writes=["rtmp"])
        P.op("ve", lambda eh: eh.reduce_sum(out=rtmp[:, 9:10], in_=rtmp[:, 0:8], axis=AX.X),
             reads=["rtmp"], writes=["rtmp"])
        P.op("ve", lambda eh: eh.reciprocal(out=rtmp[:, 10:11], in_=rtmp[:, 9:10]),
             reads=["rtmp"], writes=["rtmp"])
        P.op("ve", lambda eh: eh.tensor_scalar_mul(out=rbf[:], in0=rtmp[:, 0:8],
                                                   scalar1=rtmp[:, 10:11]),
             reads=["rtmp"], writes=["rbf"])
        P.op("te", lambda eh: eh.transpose(out=peb[0:8, 0:128], in_=rbf[:], identity=id_sb[:]),
             reads=["rbf", "ident"], writes=["peb"])
        P.op("ve", (lambda m=m: lambda eh: eh.tensor_copy(
            out=routT[0:8, m * 128:(m + 1) * 128], in_=peb[0:8, 0:128]))(),
            reads=["peb"], writes=["routT"])

    incs_b = p1c.enter_context(nc.sbuf_tensor("incs_b", [8, H], BF16))
    xst = [p1c.enter_context(nc.sbuf_tensor(f"xst{j}", [128, TOK], BF16)) for j in range(2)]
    P.op("gq", dma("gq", incs_b[:], incs[:]), writes=["incsb"], dma=True)
    for k in range(32):
        slot = k % 3
        for half in range(2):
            P.op("te", (lambda slot=slot, k=k, half=half: lambda eh: eh.matmul(
                out=pj[slot][:, half * 512:(half + 1) * 512],
                lhsT=incs_b[0:8, k * 128:(k + 1) * 128],
                rhs=routT[0:8, half * 512:(half + 1) * 512],
                start=True, stop=True))(),
                reads=["incsb", "routT"], writes=[("pj", slot)])
        j = k % 2
        P.op("ve", (lambda slot=slot, j=j, k=k: lambda eh: eh.tensor_tensor(
            out=xst[j][:], in0=xnfm[k][:], in1=pj[slot][:], op=ALU.mult))(),
            reads=[("pj", slot), ("xnfm", k)], writes=[("xst", j)])
        P.op("se", dma("se", xs_d[k * 128:(k + 1) * 128, :], xst[j][:]),
             reads=[("xst", j)], writes=["xs_d"], dma=True)
    p1c.close()
    P.fence()

    xn_load("far")
    ws = open_wscope()
    k_proj(True)
    ws.close()
    P.fence()
    v_proj(True)
    p1.close()
    ps1.close()

    def meta_write(eh):
        dst = bass.AP(meta_n, REG["mn_own"], [[1, 1], [1, 8]])
        return eh.dma_start(out=dst, in_=kn_mine[0:1, 0:8])
    P.op("gq", meta_write, reads=["knm", "regs", "kvK_w", "kvV_w"], writes=["meta_w"], dma=True)

    def flag_write(eh):
        dst = bass.AP(meta_f, REG["fl_own"], [[1, 1], [1, 1]])
        return eh.dma_start(out=dst, in_=nonce_sb[0:1, 0:1])
    P.op("gq", flag_write, reads=["meta_w", "nonce", "regs"], writes=["flag_w"], dma=True)

    poll_sem = es.enter_context(nc.semaphore("poll_sem"))

    def barrier(eh):
        g = eh
        R = REG
        g.reg_mov(R["tgt"], 16)
        g.reg_mov(R["nd"], 1)
        with g.While(R["nd"]):
            src = bass.AP(meta_f, R["fl_par"], [[1, 1], [1, 1]])
            g.dma_start(out=flag_sb[0:1, 0:1], in_=src).then_inc(poll_sem, 16)
            g.wait_ge(poll_sem, R["tgt"])
            g.reg_add(R["tgt"], R["tgt"], 16)
            g.reg_mov(R["nd"], 0)
            g.reg_load(R["fv"], flag_sb[0:1, 0:1])
            with g.If_ne(R["fv"], R["nonce"]):
                g.reg_mov(R["nd"], 1)
        return None
    P.op("gq", barrier, reads=["flag_w", "regs"], writes=["BARRIER"])

    def par_read(eh):
        src = bass.AP(meta_n, REG["mn_par"], [[1, 1], [1, 8]])
        return eh.dma_start(out=kn_par[0:1, 0:8], in_=src)
    P.op("gq", par_read, reads=["BARRIER", "regs"], writes=["knp"], dma=True)

    P.fence()
    p2 = ExitStack()
    SB2 = lambda name, shape, dt: p2.enter_context(nc.sbuf_tensor(name, shape, dt))
    attn_tm = [SB2(f"atm{i}", [128, H], BF16) for i in range(NQT)]
    Kb = [SB2(f"Kb{j}", [128, S], BF16) for j in range(2)]
    Vb = [SB2(f"Vb{j}", [128, 32, 130], BF16) for j in range(2)]
    probs = [SB2(f"probs{j}", [128, 1024], BF16) for j in range(2)]
    negc = SB2("negc", [128, 32], F32)
    kn_all = SB2("kn_all", [1, 8], F32)
    prodv = SB2("prodv", [1, 32], F32)
    negc_bf = SB2("negc_bf", [1, 32], BF16)
    rinv = SB2("rinv", [128, 8], F32)
    ps2 = ExitStack()
    psc = [ps2.enter_context(nc.psum_tensor(f"psc{j}", [128, 1024], F32)) for j in range(2)]
    ppv = [ps2.enter_context(nc.psum_tensor(f"ppv{j}", [128, 512], F32)) for j in range(3)]
    PV_SLOT = {0: (0, 0), 1: (0, 136), 2: (0, 272), 3: (1, 0), 4: (1, 136), 5: (1, 272),
               6: (2, 0), 7: (2, 136)}

    for j in range(2):
        P.op("ve", (lambda j=j: lambda eh: eh.memset(Vb[j][:, :, 128:130], 0.0))(),
             writes=[("Vb", j)])
        P.op("ve", (lambda j=j: lambda eh: eh.memset(Vb[j][:, :, 128:129], 1.0))(),
             writes=[("Vb", j)])

    P.op("ve", lambda eh: eh.tensor_max(out=kn_all[:], in0=kn_mine[0:1, 0:8],
                                        in1=kn_par[0:1, 0:8]),
         reads=["knm", "knp"], writes=["kn_all"])
    for g_ in range(4):
        P.op("ve", (lambda g_=g_: lambda eh: eh.tensor_mul(
            out=prodv[0:1, g_ * 8:(g_ + 1) * 8], in0=qn2[0:1, g_ * 8:(g_ + 1) * 8],
            in1=kn_all[0:1, 0:8]))(),
            reads=["qn2", "kn_all"], writes=["prodv"])
    P.op("se", lambda eh: eh.activation(out=prodv[:], in_=prodv[:], func=AF.Ln),
         reads=["prodv"], writes=["prodv"])
    P.op("se", lambda eh: eh.activation(out=prodv[:], in_=prodv[:], func=AF.Exp,
                                        scale=0.5, bias=lnsc_sb[0:1, 0:1]),
         reads=["prodv", "lnsc"], writes=["prodv"])
    P.op("ve", lambda eh: eh.tensor_scalar_mul(out=negc_bf[:], in0=prodv[:], scalar1=-1.0),
         reads=["prodv"], writes=["negc_bf"])
    P.op("te", lambda eh: eh.matmul(out=psc[0][:, 0:32], lhsT=ones_row[:], rhs=negc_bf[:],
                                    start=True, stop=True),
         reads=["negc_bf", "ones_row"], writes=[("psc", 0)])
    P.op("ve", lambda eh: eh.tensor_copy(out=negc[:], in_=psc[0][:, 0:32]),
         reads=[("psc", 0)], writes=["negc"])

    for hk in range(HKV):
        j = hk % 2
        for sres in range(4):
            P.op("gq", (lambda hk=hk, j=j, sres=sres: lambda eh: eh.dma_start(
                out=Kb[j][:, sres * 1024:(sres + 1) * 1024],
                in_=kvK[sres * 1024 + hk * 128: sres * 1024 + (hk + 1) * 128, :]))(),
                reads=["BARRIER"], writes=[("Kb", j)], dma=True)
        for quad in range(4):
            srcv = bass.AP(kvV, (quad * 1024) * DKV + hk * 128,
                           [[DKV, 128], [128 * DKV, 8], [1, 128]])
            P.op("gq", (lambda srcv=srcv, j=j, quad=quad: lambda eh: eh.dma_start(
                out=Vb[j][:, quad * 8:(quad + 1) * 8, 0:128], in_=srcv))(),
                reads=["BARRIER"], writes=[("Vb", j)], dma=True)
        for gg in range(GQ):
            h = hk * GQ + gg
            for v in range(32):
                i0 = v % 8
                N = (8 - i0) * 128
                jp = v % 2
                for half in range((N + 511) // 512):
                    w = min(512, N - half * 512)
                    P.op("te", (lambda jp=jp, j=j, v=v, h=h, i0=i0, half=half, w=w:
                                lambda eh: eh.matmul(
                        out=psc[jp][:, half * 512:half * 512 + w],
                        lhsT=Kb[j][:, v * 128:(v + 1) * 128],
                        rhs=Qfm[h][:, i0 * 128 + half * 512: i0 * 128 + half * 512 + w],
                        start=True, stop=True))(),
                        reads=[("Kb", j), ("Q", h)], writes=[("psc", jp)])
                P.op("se", (lambda jp=jp, N=N, h=h: lambda eh: eh.activation(
                    out=probs[jp][:, 0:N], in_=psc[jp][:, 0:N], func=AF.Exp,
                    bias=negc[:, gmaj(h):gmaj(h) + 1], scale=SM_SCALE))(),
                    reads=[("psc", jp), "negc"], writes=[("probs", jp)])
                s_idx = v // 8
                P.op("ve", (lambda jp=jp, s_idx=s_idx: lambda eh: eh.tensor_tensor(
                    out=probs[jp][:, 0:128], in0=probs[jp][:, 0:128],
                    in1=mask_sb[:, s_idx * 128:(s_idx + 1) * 128], op=ALU.mult))(),
                    reads=[("probs", jp), "mask"], writes=[("probs", jp)])
                for i in range(i0, 8):
                    bank, off = PV_SLOT[i]
                    st = (v == 0) and (i in (0, 3, 6))
                    P.op("te", (lambda jp=jp, j=j, v=v, i=i, i0=i0, bank=bank, off=off, st=st:
                                lambda eh: eh.matmul(
                        out=ppv[bank][:, off:off + 130],
                        lhsT=probs[jp][:, (i - i0) * 128:(i - i0 + 1) * 128],
                        rhs=Vb[j][:, v, 0:130],
                        start=st, stop=(v == 24 + i), skip_group_check=True))(),
                        reads=[("probs", jp), ("Vb", j)], writes=[("ppv", bank)])
            for i in range(NQT):
                bank, off = PV_SLOT[i]
                P.op("ve", (lambda bank=bank, off=off, i=i: lambda eh: eh.reciprocal(
                    out=rinv[:, i:i + 1], in_=ppv[bank][:, off + 128:off + 129]))(),
                    reads=[("ppv", bank)], writes=["rinv"])
                P.op("ve", (lambda bank=bank, off=off, i=i, h=h: lambda eh: eh.tensor_scalar_mul(
                    out=attn_tm[i][:, h * 128:(h + 1) * 128],
                    in0=ppv[bank][:, off:off + 128], scalar1=rinv[:, i:i + 1]))(),
                    reads=[("ppv", bank), "rinv"], writes=[("atm", i)])
            for i in range(NQT):
                P.op("se", (lambda i=i, h=h: lambda eh: eh.dma_start(
                    out=attn_d[i * 128:(i + 1) * 128, h * 128:(h + 1) * 128],
                    in_=attn_tm[i][:, h * 128:(h + 1) * 128]))(),
                    reads=[("atm", i)], writes=[("attn_d", h)], dma=True)
    ps2.close()
    p2.close()
    P.fence()

    p3 = ExitStack()
    SB3 = lambda name, shape, dt: p3.enter_context(nc.sbuf_tensor(name, shape, dt))
    xsfm = [SB3(f"xsfm{k}", [128, TOK], BF16) for k in range(32)]
    wo_st = SB3("wo_st", [128, 32, NB], BF16)
    wt_st = SB3("wt_st", [128, 32, NB], BF16)
    wo_f = [SB3(f"wo_f{j}", [128, 512], F32) for j in range(2)]
    s_sb = [SB3(f"s_sb{m}", [128, NB], BF16) for m in range(8)]
    bias_bc = SB3("bias_bc", [128, H], BF16)
    bias_b = SB3("bias_b", [1, H], BF16)
    ocs_b = SB3("ocs_b", [8, H], BF16)
    ys_sb = [SB3(f"ys_sb{j}", [128, NB], F32) for j in range(2)]
    t1_sb = [SB3(f"t1_sb{j}", [128, NB], F32) for j in range(2)]
    g_sb = [SB3(f"g_sb{j}", [128, NB], BF16) for j in range(2)]
    h_sb = [SB3(f"h_sb{j}", [128, NB], BF16) for j in range(2)]
    xe = [SB3(f"xe{j}", [128, NB], F32) for j in range(2)]
    oe = [SB3(f"oe{j}", [128, NB], F32) for j in range(2)]
    ps3 = ExitStack()
    pp = [ps3.enter_context(nc.psum_tensor(f"pp{i}", [128, 512], F32)) for i in range(6)]
    pos = ps3.enter_context(nc.psum_tensor("pos", [128, 512], F32))
    pbb = ps3.enter_context(nc.psum_tensor("pbb", [128, 512], F32))

    for k in range(32):
        P.op("sq", (lambda k=k: lambda eh: eh.dma_start(
            out=Qfm[k][:], in_=attn_d[0:TOK, k * 128:(k + 1) * 128], transpose=True))(),
            reads=[("attn_d", k)], writes=[("Q", k)], dma=True)
    for k in range(32):
        P.op("sq", (lambda k=k: lambda eh: eh.dma_start(
            out=xsfm[k][:], in_=xs_d[k * 128:(k + 1) * 128, :]))(),
            reads=["xs_d"], writes=[("xs", k)], dma=True)
    P.op("gq", dma("gq", bias_b[:], bias1[:]), writes=["biasb"], dma=True)
    for n8 in range(8):
        P.op("te", (lambda n8=n8: lambda eh: eh.matmul(
            out=pbb[:, 0:512], lhsT=ones_row[:], rhs=bias_b[0:1, n8 * 512:(n8 + 1) * 512],
            start=True, stop=True))(),
            reads=["biasb", "ones_row"], writes=["pbb"])
        P.op("ve", (lambda n8=n8: lambda eh: eh.tensor_copy(
            out=bias_bc[:, n8 * 512:(n8 + 1) * 512], in_=pbb[:, 0:512]))(),
            reads=["pbb"], writes=["biasbc"])
    P.op("gq", dma("gq", ocs_b[:], outcs[:]), writes=["ocsb"], dma=True)

    for n in range(NNB):
        for k2 in range(16):
            j = k2 % 2
            r0 = k2 * 256
            P.op("sq", (lambda j=j, r0=r0, n=n: lambda eh: eh.dma_start(
                out=wo_f[j][0:128, 0:256], in_=wo[r0:r0 + 128, n * NB:(n + 1) * NB]))(),
                writes=[("wo_f", j)], dma=True)
            P.op("sq", (lambda j=j, r0=r0, n=n: lambda eh: eh.dma_start(
                out=wo_f[j][0:128, 256:512], in_=wo[r0 + 128:r0 + 256, n * NB:(n + 1) * NB]))(),
                writes=[("wo_f", j)], dma=True)
            P.op("se", (lambda j=j, k2=k2: lambda eh: eh.copy(
                out=wo_st[:, 2 * k2, :], in_=wo_f[j][0:128, 0:256]))(),
                reads=[("wo_f", j)], writes=["wo_st"])
            P.op("se", (lambda j=j, k2=k2: lambda eh: eh.copy(
                out=wo_st[:, 2 * k2 + 1, :], in_=wo_f[j][0:128, 256:512]))(),
                reads=[("wo_f", j)], writes=["wo_st"])
        for m in range(8):
            slot = m % 6
            for k in range(32):
                P.op("te", (lambda slot=slot, m=m, k=k: lambda eh: eh.matmul(
                    out=pp[slot][:, 0:NB], lhsT=Qfm[k][:, m * 128:(m + 1) * 128],
                    rhs=wo_st[:, k, :], start=(k == 0), stop=(k == 31)))(),
                    reads=[("Q", k), "wo_st"], writes=[("pp", slot)])
            P.op("ve", (lambda slot=slot, m=m: lambda eh: eh.tensor_copy(
                out=s_sb[m][:], in_=pp[slot][:, 0:NB]))(),
                reads=[("pp", slot)], writes=[("s", m)])
        for k in range(32):
            P.op("sq", (lambda k=k, n=n: lambda eh: eh.dma_start(
                out=wt_st[:, k, :], in_=wTs_d[k * 128:(k + 1) * 128, n * NB:(n + 1) * NB]))(),
                reads=["wTs"], writes=["wt_st"], dma=True)
        for m in range(8):
            slot = m % 6
            for k in range(32):
                P.op("te", (lambda slot=slot, m=m, k=k: lambda eh: eh.matmul(
                    out=pp[slot][:, 0:NB], lhsT=xsfm[k][:, m * 128:(m + 1) * 128],
                    rhs=wt_st[:, k, :], start=(k == 0), stop=(k == 31)))(),
                    reads=[("xs", k), "wt_st"], writes=[("pp", slot)])
            j = m % 2
            P.op("ve", (lambda slot=slot, j=j: lambda eh: eh.tensor_copy(
                out=ys_sb[j][:], in_=pp[slot][:, 0:NB]))(),
                reads=[("pp", slot)], writes=[("ys", j)])
            P.op("te", (lambda m=m, n=n: lambda eh: eh.matmul(
                out=pos[:, 0:NB], lhsT=routT[0:8, m * 128:(m + 1) * 128],
                rhs=ocs_b[0:8, n * NB:(n + 1) * NB], start=True, stop=True))(),
                reads=["routT", "ocsb"], writes=["pos"])
            P.op("ve", (lambda j=j: lambda eh: eh.tensor_tensor(
                out=t1_sb[j][:], in0=ys_sb[j][:], in1=pos[:, 0:NB], op=ALU.mult))(),
                reads=[("ys", j), "pos"], writes=[("t1", j)])
            P.op("ve", (lambda j=j, n=n: lambda eh: eh.tensor_tensor(
                out=t1_sb[j][:], in0=t1_sb[j][:], in1=bias_bc[:, n * NB:(n + 1) * NB],
                op=ALU.add))(),
                reads=[("t1", j), "biasbc"], writes=[("t1", j)])
            P.op("se", (lambda j=j: lambda eh: eh.activation(
                out=g_sb[j][:], in_=t1_sb[j][:], func=AF.Silu))(),
                reads=[("t1", j)], writes=[("g", j)])
            P.op("ve", (lambda j=j, m=m: lambda eh: eh.tensor_tensor(
                out=h_sb[j][:], in0=g_sb[j][:], in1=s_sb[m][:], op=ALU.mult))(),
                reads=[("g", j), ("s", m)], writes=[("h", j)])
            P.op("se", (lambda j=j, m=m, n=n: lambda eh: eh.dma_start(
                out=xe[j][:], in_=x_own[m * 128:(m + 1) * 128, n * NB:(n + 1) * NB]))(),
                writes=[("xe", j)], dma=True)
            P.op("ve", (lambda j=j: lambda eh: eh.tensor_tensor(
                out=oe[j][:], in0=h_sb[j][:], in1=xe[j][:], op=ALU.add))(),
                reads=[("h", j), ("xe", j)], writes=[("oe", j)])
            P.op("se", (lambda j=j, m=m, n=n: lambda eh: eh.dma_start(
                out=out_d[m * 128:(m + 1) * 128, n * NB:(n + 1) * NB], in_=oe[j][:]))(),
                reads=[("oe", j)], writes=["out_d"], dma=True)
    ps3.close()
    p3.close()

    P.emit(nc, es, regs_init, final_waits_eng="se")
    return nc


def kernel(L0_attn_hidden_states, L0_attn_w_q, L0_attn_w_k, L0_attn_w_v, L0_attn_w_o,
           L0_ffn_weight, L0_ffn_bias, L0_ffn_gate_w, L0_ffn_in_channel_scale,
           L0_ffn_out_channel_scale):
    hs = np.ascontiguousarray(np.asarray(L0_attn_hidden_states, dtype=np.float32))
    wq = np.ascontiguousarray(np.asarray(L0_attn_w_q, dtype=np.float32))
    wk = np.ascontiguousarray(np.asarray(L0_attn_w_k, dtype=np.float32))
    wv = np.ascontiguousarray(np.asarray(L0_attn_w_v, dtype=np.float32))
    wo = np.ascontiguousarray(np.asarray(L0_attn_w_o, dtype=np.float32))
    wT = np.ascontiguousarray(np.asarray(L0_ffn_weight, dtype=np.float32).T)
    gwT = np.ascontiguousarray(np.asarray(L0_ffn_gate_w, dtype=np.float32).T)
    bias1 = np.ascontiguousarray(np.asarray(L0_ffn_bias, dtype=np.float32).reshape(1, H))
    incs = np.ascontiguousarray(np.asarray(L0_ffn_in_channel_scale, dtype=np.float32))
    outcs = np.ascontiguousarray(np.asarray(L0_ffn_out_channel_scale, dtype=np.float32))

    hs_t = hs.reshape(B, 8, 4, 128, H)
    ident = np.eye(128, dtype=np.float32).astype(ml_dtypes.bfloat16)
    nv = (int.from_bytes(os.urandom(4), 'little') & 0x7FFFFFFF) | 1
    nonce = np.full((1, 4), nv, dtype=np.int32)

    kk = np.arange(128)[:, None]
    qq = np.arange(128)[None, :]
    in_maps = []
    for c in range(8):
        b, r = c // 4, c % 4
        x_own_a = np.ascontiguousarray(hs_t[b, :, r].reshape(TOK, H))
        x_far_a = np.ascontiguousarray(hs_t[b, :, (r + 2) % 4].reshape(TOK, H))
        wm = np.zeros((128, 512), dtype=np.float32)
        for s in range(4):
            allowed = (kk - qq) <= (r - s) * 128
            wm[:, s * 128:(s + 1) * 128] = allowed.astype(np.float32)
        in_maps.append({
            "x_own": x_own_a, "x_far": x_far_a, "wq": wq, "wk": wk, "wv": wv, "wo": wo,
            "wT": wT, "gwT": gwT, "bias1": bias1, "incs": incs, "outcs": outcs,
            "wmask": wm.astype(ml_dtypes.bfloat16), "ident_in": ident,
            "nonce_in": nonce,
        })

    nc = build_program()
    res = run_bass_kernel_spmd(nc, in_maps, core_ids=list(range(8)))
    out = np.empty((B, S, H), dtype=np.float32)
    out_t = out.reshape(B, 8, 4, 128, H)
    for c in range(8):
        b, r = c // 4, c % 4
        out_t[b, :, r] = res.results[c]["out_d"].reshape(8, 128, H)
    return out


# version 22
# speedup vs baseline: 1.5684x; 1.2081x over previous
import math
import os
import numpy as np
import ml_dtypes
from contextlib import ExitStack

import concourse.bass as bass
import concourse.mybir as mybir
from concourse.bass_utils import run_bass_kernel_spmd

F32 = mybir.dt.float32
BF16 = mybir.dt.bfloat16
I32 = mybir.dt.int32
U32 = mybir.dt.uint32
AF = mybir.ActivationFunctionType
ALU = mybir.AluOpType
AX = mybir.AxisListType

B, S, H = 2, 4096, 4096
HKV, GQ, D = 8, 4, 128
DKV = HKV * D
NQT = 8
TOK = NQT * 128
SM_SCALE = 1.0 / math.sqrt(D)
NB = 256
NNB = H // NB
EPS = 1e-5

ENGS = ("te", "ve", "se", "gq", "sq")


def gmaj(h):
    return (h % GQ) * HKV + h // GQ


class Prog:

    def __init__(self):
        self.ops = []

    def op(self, eng, build, reads=(), writes=(), dma=False, xbar=False):
        assert eng in ENGS
        self.ops.append((eng, build, tuple(reads), tuple(writes), dma, xbar))

    def fence(self):
        self.ops.append(("FENCE", None, (), (), False, False))

    def emit(self, nc, es, regs_init, final_waits_eng="se"):
        cs = {e: es.enter_context(nc.semaphore(f"cs_{e}")) for e in ENGS}
        ds = {e: es.enter_context(nc.semaphore(f"ds_{e}")) for e in ("gq", "sq", "se")}
        cnt_c = {e: 0 for e in ENGS}
        cnt_d = {e: 0 for e in ("gq", "sq", "se")}
        plan = []
        last_w = {}
        readers = {}
        fence_low = {}
        expanded = []
        for entry in self.ops:
            if entry[0] == "FENCE":
                expanded.append(entry)
            else:
                expanded.append(entry)
        self.ops = expanded
        last_xbar = {}
        last_reg = {}
        for (eng, build, reads, writes, dma, xbar) in self.ops:
            if eng == "FENCE":
                plan.append(("FENCE", dict((("c", e), cnt_c[e]) for e in ENGS)
                             | dict((("d", e), cnt_d[e]) for e in ("gq", "sq", "se")), None))
                continue
            if dma:
                cnt_d[eng] += 16
                mykey, myval = ("d", eng), cnt_d[eng]
            else:
                cnt_c[eng] += 1
                mykey, myval = ("c", eng), cnt_c[eng]
            need = {}
            PSUM_KEYS = ("pj", "pd", "peb", "psc", "ppv", "pp", "pos", "pbb")

            def is_psum(b):
                t = b[0] if isinstance(b, tuple) else b
                return t in PSUM_KEYS

            def add(dep, buf=None):
                if dep is None:
                    return
                k, v = dep
                if k == mykey and v == myval:
                    return
                if (not dma) and k == mykey and eng == "te" and buf is not None \
                        and is_psum(buf):
                    return
                if need.get(k, -1) < v:
                    need[k] = v
            for b in reads:
                add(last_w.get(b), b)
            for b in writes:
                add(last_w.get(b), b)
                for dep in readers.get(b, ()):
                    add(dep, None if not is_psum(b) else b)
            if dma:
                if xbar:
                    if need.get(mykey, -1) < last_reg.get(eng, -1):
                        need[mykey] = last_reg[eng]
                    last_xbar[eng] = myval
                else:
                    if need.get(mykey, -1) < last_xbar.get(eng, -1):
                        need[mykey] = last_xbar[eng]
                    last_reg[eng] = myval
            for b in writes:
                last_w[b] = (mykey, myval)
                readers[b] = []
            for b in reads:
                readers.setdefault(b, []).append((mykey, myval))
            plan.append((mykey, myval, need))

        sem_of = {("c", e): cs[e] for e in ENGS}
        for e in ("gq", "sq", "se"):
            sem_of[("d", e)] = ds[e]
        by_eng = {e: [i for i, o in enumerate(self.ops) if o[0] in (e, "FENCE")] for e in ENGS}
        ops = self.ops
        blk = es.enter_context(nc.Block())
        satisfied = {e: {} for e in ENGS}

        def make_body(eng_name):
            idxs = by_eng[eng_name]

            def body(eh):
                if eng_name == "gq" and regs_init is not None:
                    regs_init(eh)
                sat = satisfied[eng_name]
                for i in idxs:
                    if ops[i][0] == "FENCE":
                        totals = plan[i][1]
                        for k in sorted(totals, key=repr):
                            v = totals[k]
                            if v <= 0 or sat.get(k, 0) >= v:
                                continue
                            eh.wait_ge(sem_of[k], v)
                            sat[k] = v
                        continue
                    _, build, _, _, dma, _xb = ops[i]
                    mykey, myval, need = plan[i]
                    for k in sorted(need, key=repr):
                        v = need[k]
                        if sat.get(k, 0) >= v:
                            continue
                        eh.wait_ge(sem_of[k], v)
                        sat[k] = v
                    inst = build(eh)
                    if inst is None:
                        inst = eh.nop()
                    inst.then_inc(sem_of[mykey], 16 if dma else 1)
                if eng_name == final_waits_eng:
                    for e in ("se", "sq", "gq"):
                        if cnt_d[e] > 0:
                            eh.wait_ge(ds[e], cnt_d[e])

            return body

        blk.tensor(make_body("te"))
        blk.vector(make_body("ve"))
        blk.scalar(make_body("se"))
        blk.gpsimd(make_body("gq"))
        blk.sync(make_body("sq"))


def build_program():
    nc = bass.Bass("TRN2", target_bir_lowering=False)

    x_own = nc.dram_tensor("x_own", [TOK, H], F32, kind="ExternalInput")
    x_far = nc.dram_tensor("x_far", [TOK, H], F32, kind="ExternalInput")
    wq = nc.dram_tensor("wq", [H, H], F32, kind="ExternalInput")
    wk = nc.dram_tensor("wk", [H, DKV], F32, kind="ExternalInput")
    wv = nc.dram_tensor("wv", [H, DKV], F32, kind="ExternalInput")
    wo = nc.dram_tensor("wo", [H, H], F32, kind="ExternalInput")
    wT = nc.dram_tensor("wT", [H, H], F32, kind="ExternalInput")
    gwT = nc.dram_tensor("gwT", [H, 8], F32, kind="ExternalInput")
    bias1 = nc.dram_tensor("bias1", [1, H], F32, kind="ExternalInput")
    incs = nc.dram_tensor("incs", [8, H], F32, kind="ExternalInput")
    outcs = nc.dram_tensor("outcs", [8, H], F32, kind="ExternalInput")
    wmask = nc.dram_tensor("wmask", [128, 512], BF16, kind="ExternalInput")
    ident_in = nc.dram_tensor("ident_in", [128, 128], BF16, kind="ExternalInput")
    nonce_in = nc.dram_tensor("nonce_in", [1, 4], I32, kind="ExternalInput")
    out_d = nc.dram_tensor("out_d", [TOK, H], F32, kind="ExternalOutput")

    xn_own_d = nc.dram_tensor("xn_own_d", [TOK, H], BF16)
    xn_far_d = nc.dram_tensor("xn_far_d", [TOK, H], BF16)
    xs_d = nc.dram_tensor("xs_d", [H, TOK], BF16)
    attn_d = nc.dram_tensor("attn_d", [TOK, H], BF16)
    wTs_d = nc.dram_tensor("wTs_d", [H, H], BF16)
    kvK = nc.dram_tensor("kvK", [S, 1024], BF16, addr_space="Shared")
    kvK_st = nc.dram_tensor("kvK_st", [1024, 1024], BF16)
    kvV_st = nc.dram_tensor("kvV_st", [1024, 1024], BF16)
    kvV = nc.dram_tensor("kvV", [S, DKV], BF16, addr_space="Shared")
    meta_n = nc.dram_tensor("meta_n", [2, 512], F32, addr_space="Shared")
    meta_f = nc.dram_tensor("meta_f", [2, 128], I32, addr_space="Shared")

    P = Prog()
    es = ExitStack()
    SB = lambda name, shape, dt: es.enter_context(nc.sbuf_tensor(name, shape, dt))

    Qfm = [SB(f"Qfm{m}", [128, TOK], BF16) for m in range(32)]
    mask_sb = SB("mask_sb", [128, 512], BF16)
    id_sb = SB("id_sb", [128, 128], BF16)
    ones_col = SB("ones_col", [128, 1], BF16)
    ones_row = SB("ones_row", [1, 128], BF16)
    eps_sb = SB("eps_sb", [128, 1], F32)
    lnsc_sb = SB("lnsc_sb", [1, 1], F32)
    routT = SB("routT", [8, TOK], BF16)
    qn2 = SB("qn2", [1, 32], F32)
    kn_mine = SB("kn_mine", [1, 8], F32)
    kn_par = SB("kn_par", [1, 8], F32)
    nred = SB("nred", [1, 2], F32)
    pid_sb = SB("pid_sb", [1, 1], U32)
    nonce_sb = SB("nonce_sb", [1, 4], I32)
    flag_sb = SB("flag_sb", [1, 4], I32)

    def dma(q, out, in_, **kw):
        return lambda eh: eh.dma_start(out=out, in_=in_, **kw)

    P.op("gq", dma("gq", pid_sb[:], nc.partition_id_tensor[:]), writes=["pid"], dma=True)
    P.op("gq", dma("gq", nonce_sb[:], nonce_in[:]), writes=["nonce"], dma=True)
    P.op("sq", dma("sq", mask_sb[:], wmask[:]), writes=["mask"], dma=True)
    P.op("sq", dma("sq", id_sb[:], ident_in[:]), writes=["ident"], dma=True)
    P.op("ve", lambda eh: eh.memset(ones_col[:], 1.0), writes=["ones_col"])
    P.op("ve", lambda eh: eh.memset(ones_row[:], 1.0), writes=["ones_row"])
    P.op("ve", lambda eh: eh.memset(eps_sb[:], EPS), writes=["eps"])
    P.op("ve", lambda eh: eh.memset(lnsc_sb[:], math.log(SM_SCALE)), writes=["lnsc"])

    REG = {}

    def regs_init(g):
        names = ["pid", "r", "p", "s2", "vr0", "vr1",
                 "mn_own", "mn_par", "fl_own", "fl_par", "tmp",
                 "tgt", "nd", "fv", "nonce"]
        for n in names:
            REG[n] = es.enter_context(g.register(f"rg_{n}"))

    def regs_compute(g):
        R = REG
        g.reg_load(R["pid"], pid_sb[:1, :1])
        g.reg_load(R["nonce"], nonce_sb[:1, :1])
        g.reg_mod(R["r"], R["pid"], 4)
        g.reg_mod(R["p"], R["pid"], 2)
        g.reg_add(R["s2"], R["r"], 2)
        g.reg_mod(R["s2"], R["s2"], 4)
        g.reg_mov(R["vr0"], 0); g.reg_add(R["vr0"], R["vr0"], R["r"]); g.reg_mul(R["vr0"], R["vr0"], 1024 * 1024)
        g.reg_mov(R["vr1"], 0); g.reg_add(R["vr1"], R["vr1"], R["s2"]); g.reg_mul(R["vr1"], R["vr1"], 1024 * 1024)
        g.reg_mov(R["mn_own"], 0); g.reg_add(R["mn_own"], R["mn_own"], R["p"]); g.reg_mul(R["mn_own"], R["mn_own"], 512)
        g.reg_mov(R["mn_par"], 512); g.reg_sub(R["mn_par"], R["mn_par"], R["mn_own"])
        g.reg_mov(R["fl_own"], 0); g.reg_add(R["fl_own"], R["fl_own"], R["p"]); g.reg_mul(R["fl_own"], R["fl_own"], 128)
        g.reg_mov(R["fl_par"], 128); g.reg_sub(R["fl_par"], R["fl_par"], R["fl_own"])
        return None

    P.op("gq", regs_compute, reads=["pid", "nonce"], writes=["regs"])

    p0 = ExitStack()
    SB0 = lambda name, shape, dt: p0.enter_context(nc.sbuf_tensor(name, shape, dt))
    xt = [SB0(f"p0_x{j}", [128, H], F32) for j in range(2)]
    xnt = [SB0(f"p0_xn{j}", [128, H], BF16) for j in range(2)]
    st6 = [SB0(f"p0_st6{j}", [128, 48], F32) for j in range(2)]
    mv = [SB0(f"p0_mv{j}", [128, 2], F32) for j in range(2)]
    rstd = [SB0(f"p0_rs{j}", [128, 2], F32) for j in range(2)]
    wts_f = [SB0(f"p0_wtf{j}", [128, 2048], F32) for j in range(2)]
    wts_b = [SB0(f"p0_wtb{j}", [128, 2048], BF16) for j in range(2)]

    for t in range(16):
        src = x_own if t < 8 else x_far
        dstd = xn_own_d if t < 8 else xn_far_d
        dtag = "xnod" if t < 8 else "xnfd"
        row = (t % 8) * 128
        j = t % 2
        P.op("se", dma("se", xt[j][:], src[row:row + 128, :]), writes=[("xt", j)], dma=True)
        for i in range(8):
            P.op("ve", (lambda j=j, i=i: lambda eh: eh.bn_stats(
                out=st6[j][:, i * 6:(i + 1) * 6], in_=xt[j][:, i * 512:(i + 1) * 512]))(),
                reads=[("xt", j)], writes=[("st6", j)])
        P.op("ve", (lambda j=j: lambda eh: eh.bn_aggr(out=mv[j][:], in_=st6[j][:]))(),
             reads=[("st6", j)], writes=[("mv", j)])
        P.op("se", (lambda j=j: lambda eh: eh.activation(
            out=rstd[j][:, 0:1], in_=mv[j][:, 1:2], func=AF.Ln, bias=eps_sb[:, 0:1]))(),
            reads=[("mv", j), "eps"], writes=[("rstd", j)])
        P.op("se", (lambda j=j: lambda eh: eh.activation(
            out=rstd[j][:, 1:2], in_=rstd[j][:, 0:1], func=AF.Exp, scale=-0.5))(),
            reads=[("rstd", j)], writes=[("rstd", j)])
        P.op("ve", (lambda j=j: lambda eh: eh.tensor_scalar(
            out=xnt[j][:], in0=xt[j][:], scalar1=mv[j][:, 0:1], scalar2=rstd[j][:, 1:2],
            op0=ALU.subtract, op1=ALU.mult))(),
            reads=[("xt", j), ("mv", j), ("rstd", j)], writes=[("xnt", j)])
        P.op("se", dma("se", dstd[row:row + 128, :], xnt[j][:]),
             reads=[("xnt", j)], writes=[dtag], dma=True)

    for u in range(64):
        j = u % 2
        r0 = (u // 2) * 128
        c0 = (u % 2) * 2048
        P.op("sq", dma("sq", wts_f[j][:], wT[r0:r0 + 128, c0:c0 + 2048]),
             writes=[("wtsf", j)], dma=True)
        P.op("se", (lambda j=j: lambda eh: eh.sign(out=wts_b[j][:], in_=wts_f[j][:]))(),
             reads=[("wtsf", j)], writes=[("wtsb", j)])
        P.op("sq", dma("sq", wTs_d[r0:r0 + 128, c0:c0 + 2048], wts_b[j][:]),
             reads=[("wtsb", j)], writes=["wTs"], dma=True)
    p0.close()
    P.fence()

    p1 = ExitStack()
    SB1 = lambda name, shape, dt: p1.enter_context(nc.sbuf_tensor(name, shape, dt))
    xnfm = [SB1(f"xnfm{k}", [128, TOK], BF16) for k in range(32)]
    ps1 = ExitStack()
    pj = [ps1.enter_context(nc.psum_tensor(f"pj{i}", [128, 1024], F32)) for i in range(3)]
    pd = ps1.enter_context(nc.psum_tensor("pd", [128, 512], F32))
    peb = ps1.enter_context(nc.psum_tensor("peb", [128, 256], BF16))

    def xn_load(which):
        srcd = xn_own_d if which == "own" else xn_far_d
        srctag = "xnod" if which == "own" else "xnfd"
        for k in range(32):
            P.op("sq", dma("sq", xnfm[k][:], srcd[0:TOK, k * 128:(k + 1) * 128], transpose=True),
                 reads=[srctag], writes=[("xnfm", k)], dma=True, xbar=True)

    xn_load("own")

    WS = {}

    def open_wscope():
        st = ExitStack()
        WS["stack"] = st
        WS["wcol_f"] = st.enter_context(nc.sbuf_tensor(f"wcf{next(_ctr)}", [128, 8, 256], F32))
        WS["wcol_b"] = [st.enter_context(nc.sbuf_tensor(f"wcb{next(_ctr)}", [128, 32, 256], BF16)) for j in range(2)]
        WS["sqt"] = st.enter_context(nc.sbuf_tensor(f"sqt{next(_ctr)}", [128, 1024], BF16))
        WS["kevac"] = [st.enter_context(nc.sbuf_tensor(f"kev{next(_ctr)}", [128, 1024], BF16)) for j in range(2)]
        return st

    import itertools
    _ctr = itertools.count()

    def load_wcol_pair(wten, ncols, mp, j):
        for piece in range(4):
            k0 = piece * 8
            src = bass.AP(wten, (k0 * 128) * ncols + mp * 256,
                          [[ncols, 128], [128 * ncols, 8], [1, 256]])
            P.op("sq", (lambda src=src: lambda eh: eh.dma_start(
                out=WS["wcol_f"][:], in_=src))(),
                writes=["wcf"], dma=True)
            P.op("se", (lambda j=j, k0=k0: lambda eh: eh.copy(
                out=WS["wcol_b"][j][:, k0:k0 + 8, :], in_=WS["wcol_f"][:]))(),
                reads=["wcf"], writes=[("wcb", j)])

    def col_norm_max(src_bf, srckey, dst_ap, dstkey):
        sqt = WS["sqt"]
        P.op("se", (lambda sqt=sqt, src_bf=src_bf: lambda eh: eh.activation(
            out=sqt[:], in_=src_bf[:], func=AF.Square))(),
            reads=[srckey], writes=["sqt"])
        for half in range(2):
            P.op("te", (lambda half=half, sqt=sqt: lambda eh: eh.matmul(
                out=pd[0:1, 0:512], lhsT=ones_col[:], rhs=sqt[:, half * 512:(half + 1) * 512],
                start=True, stop=True))(),
                reads=["sqt", "ones_col"], writes=["pd"])
            P.op("ve", (lambda half=half: lambda eh: eh.reduce_max(
                out=nred[0:1, half:half + 1], in_=pd[0:1, 0:512], axis=AX.X))(),
                reads=["pd"], writes=["nred"])
        P.op("ve", lambda eh: eh.tensor_max(out=nred[0:1, 0:1], in0=nred[0:1, 0:1],
                                            in1=nred[0:1, 1:2]),
             reads=["nred"], writes=["nred"])
        P.op("ve", (lambda dst_ap=dst_ap: lambda eh: eh.tensor_max(
            out=dst_ap, in0=dst_ap, in1=nred[0:1, 0:1]))(),
            reads=["nred", dstkey], writes=[dstkey])

    P.op("ve", lambda eh: eh.memset(qn2[:], 0.0), writes=["qn2"])
    P.op("ve", lambda eh: eh.memset(kn_mine[:], 0.0), writes=["knm"])

    def q_proj():
        for mp in range(16):
            j = mp % 2
            load_wcol_pair(wq, H, mp, j)
            for mh in range(2):
                m = mp * 2 + mh
                slot = m % 3
                for k in range(32):
                    for half in range(2):
                        P.op("te", (lambda slot=slot, j=j, k=k, half=half, mh=mh: lambda eh: eh.matmul(
                            out=pj[slot][:, half * 512:(half + 1) * 512],
                            lhsT=WS["wcol_b"][j][:, k, mh * 128:(mh + 1) * 128],
                            rhs=xnfm[k][:, half * 512:(half + 1) * 512],
                            start=(k == 0), stop=(k == 31)))(),
                            reads=[("wcb", j), ("xnfm", k)], writes=[("pj", slot)])
                P.op("ve", (lambda m=m, slot=slot: lambda eh: eh.tensor_copy(
                    out=Qfm[m][:], in_=pj[slot][:]))(),
                    reads=[("pj", slot)], writes=[("Q", m)])
                col_norm_max(Qfm[m], ("Q", m), qn2[0:1, gmaj(m):gmaj(m) + 1], "qn2")

    def k_proj(pass_far):
        creg = "vr1" if pass_far else "vr0"
        for hk in range(HKV):
            if hk % 2 == 0:
                load_wcol_pair(wk, DKV, hk // 2, (hk // 2) % 2)
            j = (hk // 2) % 2
            slot = hk % 3
            for k in range(32):
                for half in range(2):
                    P.op("te", (lambda slot=slot, j=j, k=k, half=half, hh=hk % 2: lambda eh: eh.matmul(
                        out=pj[slot][:, half * 512:(half + 1) * 512],
                        lhsT=WS["wcol_b"][j][:, k, hh * 128:(hh + 1) * 128],
                        rhs=xnfm[k][:, half * 512:(half + 1) * 512],
                        start=(k == 0), stop=(k == 31)))(),
                        reads=[("wcb", j), ("xnfm", k)], writes=[("pj", slot)])
            je = hk % 2
            kv_t = WS["kevac"][je]
            P.op("ve", (lambda slot=slot, kv_t=kv_t: lambda eh: eh.tensor_copy(
                out=kv_t[:], in_=pj[slot][:]))(),
                reads=[("pj", slot)], writes=[("kev", je)])
            col_norm_max(kv_t, ("kev", je), kn_mine[0:1, hk:hk + 1], "knm")

            P.op("sq", (lambda hk=hk, kv_t=kv_t: lambda eh: eh.dma_start(
                out=kvK_st[hk * 128:(hk + 1) * 128, :], in_=kv_t[:]))(),
                reads=[("kev", je), "kvK_cp"], writes=["kvKst"], dma=True)

        def k_copy(eh, creg=creg):
            g = eh
            dst = bass.AP(kvK, REG[creg], [[1024, 1024], [1, 1024]])
            return g.dma_start(out=dst, in_=kvK_st[:])
        P.op("gq", k_copy, reads=["kvKst", "regs"], writes=["kvK_w", "kvK_cp"], dma=True)

    def v_proj(pass_far):
        vreg = "vr1" if pass_far else "vr0"
        p1b = ExitStack()
        tag = "f" if pass_far else "o"
        wvf = p1b.enter_context(nc.sbuf_tensor(f"wvf{tag}", [128, 8, 512], F32))
        wvb = p1b.enter_context(nc.sbuf_tensor(f"wvb{tag}", [128, 32, 512], BF16))
        vev8 = [p1b.enter_context(nc.sbuf_tensor(f"vev{tag}{mt}", [128, 1024], BF16)) for mt in range(8)]
        for half in range(2):
            for piece in range(4):
                k0 = piece * 8
                src = bass.AP(wv, (k0 * 128) * DKV + half * 512,
                              [[DKV, 128], [128 * DKV, 8], [1, 512]])
                P.op("sq", (lambda src=src: lambda eh: eh.dma_start(out=wvf[:], in_=src))(),
                     writes=["wvf"], dma=True)
                P.op("se", (lambda k0=k0: lambda eh: eh.copy(out=wvb[:, k0:k0 + 8, :], in_=wvf[:]))(),
                     reads=["wvf"], writes=["wvb"])
            for mt in range(8):
                slot = mt % 3
                for k in range(32):
                    P.op("te", (lambda slot=slot, k=k, mt=mt: lambda eh: eh.matmul(
                        out=pj[slot][:, 0:512],
                        lhsT=xnfm[k][:, mt * 128:(mt + 1) * 128],
                        rhs=wvb[:, k, :],
                        start=(k == 0), stop=(k == 31)))(),
                        reads=["wvb", ("xnfm", k)], writes=[("pj", slot)])
                P.op("ve", (lambda slot=slot, mt=mt, half=half: lambda eh: eh.tensor_copy(
                    out=vev8[mt][:, half * 512:(half + 1) * 512], in_=pj[slot][:, 0:512]))(),
                    reads=[("pj", slot)], writes=[("vev", mt)])
        for mt in range(8):
            P.op("sq", (lambda mt=mt, vt=vev8[mt]: lambda eh: eh.dma_start(
                out=kvV_st[mt * 128:(mt + 1) * 128, :], in_=vt[:]))(),
                reads=[("vev", mt), "kvV_cp"], writes=["kvVst"], dma=True)

        def v_copy(eh, vreg=vreg):
            g = eh
            dst = bass.AP(kvV, REG[vreg], [[1024, 1024], [1, 1024]])
            return g.dma_start(out=dst, in_=kvV_st[:])
        P.op("gq", v_copy, reads=["kvVst", "regs"], writes=["kvV_w", "kvV_cp"], dma=True)
        p1b.close()
        P.fence()

    ws = open_wscope()
    q_proj()
    k_proj(False)
    ws.close()
    P.fence()
    v_proj(False)

    p1c = ExitStack()
    gwb = p1c.enter_context(nc.sbuf_tensor("gwb", [128, 32, 8], BF16))
    rtmp = p1c.enter_context(nc.sbuf_tensor("rtmp", [128, 12], F32))
    rbf = p1c.enter_context(nc.sbuf_tensor("rbf", [128, 8], BF16))
    gsrc = bass.AP(gwT, 0, [[8, 128], [1024, 32], [1, 8]])
    P.op("gq", dma("gq", gwb[:], gsrc, allow_slow_non_contiguous=True), writes=["gwb"], dma=True)
    for m in range(8):
        for k in range(32):
            P.op("te", (lambda m=m, k=k: lambda eh: eh.matmul(
                out=pd[:, 0:8], lhsT=xnfm[k][:, m * 128:(m + 1) * 128], rhs=gwb[:, k, :],
                start=(k == 0), stop=(k == 31)))(),
                reads=["gwb", ("xnfm", k)], writes=["pd"])
        P.op("ve", lambda eh: eh.reduce_max(out=rtmp[:, 8:9], in_=pd[:, 0:8], axis=AX.X,
                                            negate=True),
             reads=["pd"], writes=["rtmp"])
        P.op("se", lambda eh: eh.activation(out=rtmp[:, 0:8], in_=pd[:, 0:8], func=AF.Exp,
                                            bias=rtmp[:, 8:9]),
             reads=["pd", "rtmp"], writes=["rtmp"])
        P.op("ve", lambda eh: eh.reduce_sum(out=rtmp[:, 9:10], in_=rtmp[:, 0:8], axis=AX.X),
             reads=["rtmp"], writes=["rtmp"])
        P.op("ve", lambda eh: eh.reciprocal(out=rtmp[:, 10:11], in_=rtmp[:, 9:10]),
             reads=["rtmp"], writes=["rtmp"])
        P.op("ve", lambda eh: eh.tensor_scalar_mul(out=rbf[:], in0=rtmp[:, 0:8],
                                                   scalar1=rtmp[:, 10:11]),
             reads=["rtmp"], writes=["rbf"])
        P.op("te", lambda eh: eh.transpose(out=peb[0:8, 0:128], in_=rbf[:], identity=id_sb[:]),
             reads=["rbf", "ident"], writes=["peb"])
        P.op("ve", (lambda m=m: lambda eh: eh.tensor_copy(
            out=routT[0:8, m * 128:(m + 1) * 128], in_=peb[0:8, 0:128]))(),
            reads=["peb"], writes=["routT"])

    incs_b = p1c.enter_context(nc.sbuf_tensor("incs_b", [8, H], BF16))
    xst = [p1c.enter_context(nc.sbuf_tensor(f"xst{j}", [128, TOK], BF16)) for j in range(2)]
    P.op("gq", dma("gq", incs_b[:], incs[:]), writes=["incsb"], dma=True)
    for k in range(32):
        slot = k % 3
        for half in range(2):
            P.op("te", (lambda slot=slot, k=k, half=half: lambda eh: eh.matmul(
                out=pj[slot][:, half * 512:(half + 1) * 512],
                lhsT=incs_b[0:8, k * 128:(k + 1) * 128],
                rhs=routT[0:8, half * 512:(half + 1) * 512],
                start=True, stop=True))(),
                reads=["incsb", "routT"], writes=[("pj", slot)])
        j = k % 2
        P.op("ve", (lambda slot=slot, j=j, k=k: lambda eh: eh.tensor_tensor(
            out=xst[j][:], in0=xnfm[k][:], in1=pj[slot][:], op=ALU.mult))(),
            reads=[("pj", slot), ("xnfm", k)], writes=[("xst", j)])
        P.op("se", dma("se", xs_d[k * 128:(k + 1) * 128, :], xst[j][:]),
             reads=[("xst", j)], writes=["xs_d"], dma=True)
    p1c.close()
    P.fence()

    xn_load("far")
    ws = open_wscope()
    k_proj(True)
    ws.close()
    P.fence()
    v_proj(True)
    p1.close()
    ps1.close()

    def meta_write(eh):
        dst = bass.AP(meta_n, REG["mn_own"], [[1, 1], [1, 8]])
        return eh.dma_start(out=dst, in_=kn_mine[0:1, 0:8])
    P.op("gq", meta_write, reads=["knm", "regs", "kvK_w", "kvV_w"], writes=["meta_w"], dma=True)

    def flag_write(eh):
        dst = bass.AP(meta_f, REG["fl_own"], [[1, 1], [1, 1]])
        return eh.dma_start(out=dst, in_=nonce_sb[0:1, 0:1])
    P.op("gq", flag_write, reads=["meta_w", "nonce", "regs"], writes=["flag_w"], dma=True)

    poll_sem = es.enter_context(nc.semaphore("poll_sem"))

    def barrier(eh):
        g = eh
        R = REG
        g.reg_mov(R["tgt"], 16)
        g.reg_mov(R["nd"], 1)
        with g.While(R["nd"]):
            src = bass.AP(meta_f, R["fl_par"], [[1, 1], [1, 1]])
            g.dma_start(out=flag_sb[0:1, 0:1], in_=src).then_inc(poll_sem, 16)
            g.wait_ge(poll_sem, R["tgt"])
            g.reg_add(R["tgt"], R["tgt"], 16)
            g.reg_mov(R["nd"], 0)
            g.reg_load(R["fv"], flag_sb[0:1, 0:1])
            with g.If_ne(R["fv"], R["nonce"]):
                g.reg_mov(R["nd"], 1)
        return None
    P.op("gq", barrier, reads=["flag_w", "regs"], writes=["BARRIER"])

    def par_read(eh):
        src = bass.AP(meta_n, REG["mn_par"], [[1, 1], [1, 8]])
        return eh.dma_start(out=kn_par[0:1, 0:8], in_=src)
    P.op("gq", par_read, reads=["BARRIER", "regs"], writes=["knp"], dma=True)

    P.fence()
    p2 = ExitStack()
    SB2 = lambda name, shape, dt: p2.enter_context(nc.sbuf_tensor(name, shape, dt))
    attn_tm = [SB2(f"atm{i}", [128, H], BF16) for i in range(NQT)]
    Kb = [SB2(f"Kb{j}", [128, S], BF16) for j in range(2)]
    Vb = [SB2(f"Vb{j}", [128, 32, 130], BF16) for j in range(2)]
    probs = [SB2(f"probs{j}", [128, 1024], BF16) for j in range(2)]
    negc = SB2("negc", [128, 32], F32)
    kn_all = SB2("kn_all", [1, 8], F32)
    prodv = SB2("prodv", [1, 32], F32)
    negc_bf = SB2("negc_bf", [1, 32], BF16)
    rinv = SB2("rinv", [128, 8], F32)
    ps2 = ExitStack()
    psc = [ps2.enter_context(nc.psum_tensor(f"psc{j}", [128, 1024], F32)) for j in range(2)]
    ppv = [ps2.enter_context(nc.psum_tensor(f"ppv{j}", [128, 512], F32)) for j in range(3)]
    PV_SLOT = {0: (0, 0), 1: (0, 136), 2: (0, 272), 3: (1, 0), 4: (1, 136), 5: (1, 272),
               6: (2, 0), 7: (2, 136)}

    for j in range(2):
        P.op("ve", (lambda j=j: lambda eh: eh.memset(Vb[j][:, :, 128:130], 0.0))(),
             writes=[("Vb", j)])
        P.op("ve", (lambda j=j: lambda eh: eh.memset(Vb[j][:, :, 128:129], 1.0))(),
             writes=[("Vb", j)])

    P.op("ve", lambda eh: eh.tensor_max(out=kn_all[:], in0=kn_mine[0:1, 0:8],
                                        in1=kn_par[0:1, 0:8]),
         reads=["knm", "knp"], writes=["kn_all"])
    for g_ in range(4):
        P.op("ve", (lambda g_=g_: lambda eh: eh.tensor_mul(
            out=prodv[0:1, g_ * 8:(g_ + 1) * 8], in0=qn2[0:1, g_ * 8:(g_ + 1) * 8],
            in1=kn_all[0:1, 0:8]))(),
            reads=["qn2", "kn_all"], writes=["prodv"])
    P.op("se", lambda eh: eh.activation(out=prodv[:], in_=prodv[:], func=AF.Ln),
         reads=["prodv"], writes=["prodv"])
    P.op("se", lambda eh: eh.activation(out=prodv[:], in_=prodv[:], func=AF.Exp,
                                        scale=0.5, bias=lnsc_sb[0:1, 0:1]),
         reads=["prodv", "lnsc"], writes=["prodv"])
    P.op("ve", lambda eh: eh.tensor_scalar_mul(out=negc_bf[:], in0=prodv[:], scalar1=-1.0),
         reads=["prodv"], writes=["negc_bf"])
    P.op("te", lambda eh: eh.matmul(out=psc[0][:, 0:32], lhsT=ones_row[:], rhs=negc_bf[:],
                                    start=True, stop=True),
         reads=["negc_bf", "ones_row"], writes=[("psc", 0)])
    P.op("ve", lambda eh: eh.tensor_copy(out=negc[:], in_=psc[0][:, 0:32]),
         reads=[("psc", 0)], writes=["negc"])

    for hk in range(HKV):
        j = hk % 2
        for sres in range(4):
            P.op("gq", (lambda hk=hk, j=j, sres=sres: lambda eh: eh.dma_start(
                out=Kb[j][:, sres * 1024:(sres + 1) * 1024],
                in_=kvK[sres * 1024 + hk * 128: sres * 1024 + (hk + 1) * 128, :]))(),
                reads=["BARRIER"], writes=[("Kb", j)], dma=True)
        for quad in range(4):
            srcv = bass.AP(kvV, (quad * 1024) * DKV + hk * 128,
                           [[DKV, 128], [128 * DKV, 8], [1, 128]])
            P.op("gq", (lambda srcv=srcv, j=j, quad=quad: lambda eh: eh.dma_start(
                out=Vb[j][:, quad * 8:(quad + 1) * 8, 0:128], in_=srcv))(),
                reads=["BARRIER"], writes=[("Vb", j)], dma=True)
        for gg in range(GQ):
            h = hk * GQ + gg
            for v in range(32):
                i0 = v % 8
                N = (8 - i0) * 128
                jp = v % 2
                for half in range((N + 511) // 512):
                    w = min(512, N - half * 512)
                    P.op("te", (lambda jp=jp, j=j, v=v, h=h, i0=i0, half=half, w=w:
                                lambda eh: eh.matmul(
                        out=psc[jp][:, half * 512:half * 512 + w],
                        lhsT=Kb[j][:, v * 128:(v + 1) * 128],
                        rhs=Qfm[h][:, i0 * 128 + half * 512: i0 * 128 + half * 512 + w],
                        start=True, stop=True))(),
                        reads=[("Kb", j), ("Q", h)], writes=[("psc", jp)])
                P.op("se", (lambda jp=jp, N=N, h=h: lambda eh: eh.activation(
                    out=probs[jp][:, 0:N], in_=psc[jp][:, 0:N], func=AF.Exp,
                    bias=negc[:, gmaj(h):gmaj(h) + 1], scale=SM_SCALE))(),
                    reads=[("psc", jp), "negc"], writes=[("probs", jp)])
                s_idx = v // 8
                P.op("ve", (lambda jp=jp, s_idx=s_idx: lambda eh: eh.tensor_tensor(
                    out=probs[jp][:, 0:128], in0=probs[jp][:, 0:128],
                    in1=mask_sb[:, s_idx * 128:(s_idx + 1) * 128], op=ALU.mult))(),
                    reads=[("probs", jp), "mask"], writes=[("probs", jp)])
                for i in range(i0, 8):
                    bank, off = PV_SLOT[i]
                    st = (v == 0) and (i in (0, 3, 6))
                    P.op("te", (lambda jp=jp, j=j, v=v, i=i, i0=i0, bank=bank, off=off, st=st:
                                lambda eh: eh.matmul(
                        out=ppv[bank][:, off:off + 130],
                        lhsT=probs[jp][:, (i - i0) * 128:(i - i0 + 1) * 128],
                        rhs=Vb[j][:, v, 0:130],
                        start=st, stop=(v == 24 + i), skip_group_check=True))(),
                        reads=[("probs", jp), ("Vb", j)], writes=[("ppv", bank)])
            for i in range(NQT):
                bank, off = PV_SLOT[i]
                P.op("ve", (lambda bank=bank, off=off, i=i: lambda eh: eh.reciprocal(
                    out=rinv[:, i:i + 1], in_=ppv[bank][:, off + 128:off + 129]))(),
                    reads=[("ppv", bank)], writes=["rinv"])
                P.op("ve", (lambda bank=bank, off=off, i=i, h=h: lambda eh: eh.tensor_scalar_mul(
                    out=attn_tm[i][:, h * 128:(h + 1) * 128],
                    in0=ppv[bank][:, off:off + 128], scalar1=rinv[:, i:i + 1]))(),
                    reads=[("ppv", bank), "rinv"], writes=[("atm", i)])
            for i in range(NQT):
                P.op("se", (lambda i=i, h=h: lambda eh: eh.dma_start(
                    out=attn_d[i * 128:(i + 1) * 128, h * 128:(h + 1) * 128],
                    in_=attn_tm[i][:, h * 128:(h + 1) * 128]))(),
                    reads=[("atm", i)], writes=[("attn_d", h)], dma=True)
    ps2.close()
    p2.close()
    P.fence()

    p3 = ExitStack()
    SB3 = lambda name, shape, dt: p3.enter_context(nc.sbuf_tensor(name, shape, dt))
    xsfm = [SB3(f"xsfm{k}", [128, TOK], BF16) for k in range(32)]
    wo_st = SB3("wo_st", [128, 32, NB], BF16)
    wt_st = SB3("wt_st", [128, 32, NB], BF16)
    wo_f = [SB3(f"wo_f{j}", [128, 512], F32) for j in range(2)]
    s_sb = [SB3(f"s_sb{m}", [128, NB], BF16) for m in range(8)]
    bias_bc = SB3("bias_bc", [128, H], BF16)
    bias_b = SB3("bias_b", [1, H], BF16)
    ocs_b = SB3("ocs_b", [8, H], BF16)
    ys_sb = [SB3(f"ys_sb{j}", [128, NB], F32) for j in range(2)]
    t1_sb = [SB3(f"t1_sb{j}", [128, NB], F32) for j in range(2)]
    g_sb = [SB3(f"g_sb{j}", [128, NB], BF16) for j in range(2)]
    h_sb = [SB3(f"h_sb{j}", [128, NB], BF16) for j in range(2)]
    xe = [SB3(f"xe{j}", [128, NB], F32) for j in range(2)]
    oe = [SB3(f"oe{j}", [128, NB], F32) for j in range(2)]
    ps3 = ExitStack()
    pp = [ps3.enter_context(nc.psum_tensor(f"pp{i}", [128, 512], F32)) for i in range(6)]
    pos = ps3.enter_context(nc.psum_tensor("pos", [128, 512], F32))
    pbb = ps3.enter_context(nc.psum_tensor("pbb", [128, 512], F32))

    for k in range(32):
        P.op("sq", (lambda k=k: lambda eh: eh.dma_start(
            out=Qfm[k][:], in_=attn_d[0:TOK, k * 128:(k + 1) * 128], transpose=True))(),
            reads=[("attn_d", k)], writes=[("Q", k)], dma=True, xbar=True)
    for k in range(32):
        P.op("sq", (lambda k=k: lambda eh: eh.dma_start(
            out=xsfm[k][:], in_=xs_d[k * 128:(k + 1) * 128, :]))(),
            reads=["xs_d"], writes=[("xs", k)], dma=True)
    P.op("gq", dma("gq", bias_b[:], bias1[:]), writes=["biasb"], dma=True)
    for n8 in range(8):
        P.op("te", (lambda n8=n8: lambda eh: eh.matmul(
            out=pbb[:, 0:512], lhsT=ones_row[:], rhs=bias_b[0:1, n8 * 512:(n8 + 1) * 512],
            start=True, stop=True))(),
            reads=["biasb", "ones_row"], writes=["pbb"])
        P.op("ve", (lambda n8=n8: lambda eh: eh.tensor_copy(
            out=bias_bc[:, n8 * 512:(n8 + 1) * 512], in_=pbb[:, 0:512]))(),
            reads=["pbb"], writes=["biasbc"])
    P.op("gq", dma("gq", ocs_b[:], outcs[:]), writes=["ocsb"], dma=True)

    for n in range(NNB):
        for k2 in range(16):
            j = k2 % 2
            r0 = k2 * 256
            P.op("sq", (lambda j=j, r0=r0, n=n: lambda eh: eh.dma_start(
                out=wo_f[j][0:128, 0:256], in_=wo[r0:r0 + 128, n * NB:(n + 1) * NB]))(),
                writes=[("wo_f", j)], dma=True)
            P.op("sq", (lambda j=j, r0=r0, n=n: lambda eh: eh.dma_start(
                out=wo_f[j][0:128, 256:512], in_=wo[r0 + 128:r0 + 256, n * NB:(n + 1) * NB]))(),
                writes=[("wo_f", j)], dma=True)
            P.op("se", (lambda j=j, k2=k2: lambda eh: eh.copy(
                out=wo_st[:, 2 * k2, :], in_=wo_f[j][0:128, 0:256]))(),
                reads=[("wo_f", j)], writes=["wo_st"])
            P.op("se", (lambda j=j, k2=k2: lambda eh: eh.copy(
                out=wo_st[:, 2 * k2 + 1, :], in_=wo_f[j][0:128, 256:512]))(),
                reads=[("wo_f", j)], writes=["wo_st"])
        for m in range(8):
            slot = m % 6
            for k in range(32):
                P.op("te", (lambda slot=slot, m=m, k=k: lambda eh: eh.matmul(
                    out=pp[slot][:, 0:NB], lhsT=Qfm[k][:, m * 128:(m + 1) * 128],
                    rhs=wo_st[:, k, :], start=(k == 0), stop=(k == 31)))(),
                    reads=[("Q", k), "wo_st"], writes=[("pp", slot)])
            P.op("ve", (lambda slot=slot, m=m: lambda eh: eh.tensor_copy(
                out=s_sb[m][:], in_=pp[slot][:, 0:NB]))(),
                reads=[("pp", slot)], writes=[("s", m)])
        for k in range(32):
            P.op("sq", (lambda k=k, n=n: lambda eh: eh.dma_start(
                out=wt_st[:, k, :], in_=wTs_d[k * 128:(k + 1) * 128, n * NB:(n + 1) * NB]))(),
                reads=["wTs"], writes=["wt_st"], dma=True)
        for m in range(8):
            slot = m % 6
            for k in range(32):
                P.op("te", (lambda slot=slot, m=m, k=k: lambda eh: eh.matmul(
                    out=pp[slot][:, 0:NB], lhsT=xsfm[k][:, m * 128:(m + 1) * 128],
                    rhs=wt_st[:, k, :], start=(k == 0), stop=(k == 31)))(),
                    reads=[("xs", k), "wt_st"], writes=[("pp", slot)])
            j = m % 2
            P.op("ve", (lambda slot=slot, j=j: lambda eh: eh.tensor_copy(
                out=ys_sb[j][:], in_=pp[slot][:, 0:NB]))(),
                reads=[("pp", slot)], writes=[("ys", j)])
            P.op("te", (lambda m=m, n=n: lambda eh: eh.matmul(
                out=pos[:, 0:NB], lhsT=routT[0:8, m * 128:(m + 1) * 128],
                rhs=ocs_b[0:8, n * NB:(n + 1) * NB], start=True, stop=True))(),
                reads=["routT", "ocsb"], writes=["pos"])
            P.op("ve", (lambda j=j: lambda eh: eh.tensor_tensor(
                out=t1_sb[j][:], in0=ys_sb[j][:], in1=pos[:, 0:NB], op=ALU.mult))(),
                reads=[("ys", j), "pos"], writes=[("t1", j)])
            P.op("ve", (lambda j=j, n=n: lambda eh: eh.tensor_tensor(
                out=t1_sb[j][:], in0=t1_sb[j][:], in1=bias_bc[:, n * NB:(n + 1) * NB],
                op=ALU.add))(),
                reads=[("t1", j), "biasbc"], writes=[("t1", j)])
            P.op("se", (lambda j=j: lambda eh: eh.activation(
                out=g_sb[j][:], in_=t1_sb[j][:], func=AF.Silu))(),
                reads=[("t1", j)], writes=[("g", j)])
            P.op("ve", (lambda j=j, m=m: lambda eh: eh.tensor_tensor(
                out=h_sb[j][:], in0=g_sb[j][:], in1=s_sb[m][:], op=ALU.mult))(),
                reads=[("g", j), ("s", m)], writes=[("h", j)])
            P.op("se", (lambda j=j, m=m, n=n: lambda eh: eh.dma_start(
                out=xe[j][:], in_=x_own[m * 128:(m + 1) * 128, n * NB:(n + 1) * NB]))(),
                writes=[("xe", j)], dma=True)
            P.op("ve", (lambda j=j: lambda eh: eh.tensor_tensor(
                out=oe[j][:], in0=h_sb[j][:], in1=xe[j][:], op=ALU.add))(),
                reads=[("h", j), ("xe", j)], writes=[("oe", j)])
            P.op("se", (lambda j=j, m=m, n=n: lambda eh: eh.dma_start(
                out=out_d[m * 128:(m + 1) * 128, n * NB:(n + 1) * NB], in_=oe[j][:]))(),
                reads=[("oe", j)], writes=["out_d"], dma=True)
    ps3.close()
    p3.close()

    P.emit(nc, es, regs_init, final_waits_eng="se")
    return nc


def kernel(L0_attn_hidden_states, L0_attn_w_q, L0_attn_w_k, L0_attn_w_v, L0_attn_w_o,
           L0_ffn_weight, L0_ffn_bias, L0_ffn_gate_w, L0_ffn_in_channel_scale,
           L0_ffn_out_channel_scale):
    hs = np.ascontiguousarray(np.asarray(L0_attn_hidden_states, dtype=np.float32))
    wq = np.ascontiguousarray(np.asarray(L0_attn_w_q, dtype=np.float32))
    wk = np.ascontiguousarray(np.asarray(L0_attn_w_k, dtype=np.float32))
    wv = np.ascontiguousarray(np.asarray(L0_attn_w_v, dtype=np.float32))
    wo = np.ascontiguousarray(np.asarray(L0_attn_w_o, dtype=np.float32))
    wT = np.ascontiguousarray(np.asarray(L0_ffn_weight, dtype=np.float32).T)
    gwT = np.ascontiguousarray(np.asarray(L0_ffn_gate_w, dtype=np.float32).T)
    bias1 = np.ascontiguousarray(np.asarray(L0_ffn_bias, dtype=np.float32).reshape(1, H))
    incs = np.ascontiguousarray(np.asarray(L0_ffn_in_channel_scale, dtype=np.float32))
    outcs = np.ascontiguousarray(np.asarray(L0_ffn_out_channel_scale, dtype=np.float32))

    hs_t = hs.reshape(B, 8, 4, 128, H)
    ident = np.eye(128, dtype=np.float32).astype(ml_dtypes.bfloat16)
    nv = (int.from_bytes(os.urandom(4), 'little') & 0x7FFFFFFF) | 1
    nonce = np.full((1, 4), nv, dtype=np.int32)

    kk = np.arange(128)[:, None]
    qq = np.arange(128)[None, :]
    in_maps = []
    for c in range(8):
        b, r = c // 4, c % 4
        x_own_a = np.ascontiguousarray(hs_t[b, :, r].reshape(TOK, H))
        x_far_a = np.ascontiguousarray(hs_t[b, :, (r + 2) % 4].reshape(TOK, H))
        wm = np.zeros((128, 512), dtype=np.float32)
        for s in range(4):
            allowed = (kk - qq) <= (r - s) * 128
            wm[:, s * 128:(s + 1) * 128] = allowed.astype(np.float32)
        in_maps.append({
            "x_own": x_own_a, "x_far": x_far_a, "wq": wq, "wk": wk, "wv": wv, "wo": wo,
            "wT": wT, "gwT": gwT, "bias1": bias1, "incs": incs, "outcs": outcs,
            "wmask": wm.astype(ml_dtypes.bfloat16), "ident_in": ident,
            "nonce_in": nonce,
        })

    nc = build_program()
    res = run_bass_kernel_spmd(nc, in_maps, core_ids=list(range(8)))
    out = np.empty((B, S, H), dtype=np.float32)
    out_t = out.reshape(B, 8, 4, 128, H)
    for c in range(8):
        b, r = c // 4, c % 4
        out_t[b, :, r] = res.results[c]["out_d"].reshape(8, 128, H)
    return out


# version 24 (final)
# speedup vs baseline: 1.8812x; 1.1995x over previous
import math
import os
import numpy as np
import ml_dtypes
from contextlib import ExitStack

import concourse.bass as bass
import concourse.mybir as mybir
from concourse.bass_utils import run_bass_kernel_spmd

F32 = mybir.dt.float32
BF16 = mybir.dt.bfloat16
I32 = mybir.dt.int32
U32 = mybir.dt.uint32
AF = mybir.ActivationFunctionType
ALU = mybir.AluOpType
AX = mybir.AxisListType

B, S, H = 2, 4096, 4096
HKV, GQ, D = 8, 4, 128
DKV = HKV * D
NQT = 8
TOK = NQT * 128
SM_SCALE = 1.0 / math.sqrt(D)
NB = 256
NNB = H // NB
EPS = 1e-5

ENGS = ("te", "ve", "se", "gq", "sq")


def gmaj(h):
    return (h % GQ) * HKV + h // GQ


class Prog:

    def __init__(self):
        self.ops = []

    def op(self, eng, build, reads=(), writes=(), dma=False, xbar=False):
        assert eng in ENGS
        self.ops.append((eng, build, tuple(reads), tuple(writes), dma, xbar))

    def fence(self):
        self.ops.append(("FENCE", None, (), (), False, False))

    def emit(self, nc, es, regs_init, final_waits_eng="se"):
        cs = {e: es.enter_context(nc.semaphore(f"cs_{e}")) for e in ENGS}
        ds = {e: es.enter_context(nc.semaphore(f"ds_{e}")) for e in ("gq", "sq", "se")}
        cnt_c = {e: 0 for e in ENGS}
        cnt_d = {e: 0 for e in ("gq", "sq", "se")}
        plan = []
        last_w = {}
        readers = {}
        fence_low = {}
        expanded = []
        for entry in self.ops:
            if entry[0] == "FENCE":
                expanded.append(entry)
            else:
                expanded.append(entry)
        self.ops = expanded
        last_xbar = {}
        last_reg = {}
        for (eng, build, reads, writes, dma, xbar) in self.ops:
            if eng == "FENCE":
                plan.append(("FENCE", dict((("c", e), cnt_c[e]) for e in ENGS)
                             | dict((("d", e), cnt_d[e]) for e in ("gq", "sq", "se")), None))
                continue
            if dma:
                cnt_d[eng] += 16
                mykey, myval = ("d", eng), cnt_d[eng]
            else:
                cnt_c[eng] += 1
                mykey, myval = ("c", eng), cnt_c[eng]
            need = {}
            PSUM_KEYS = ("pj", "pd", "peb", "psc", "ppv", "pp", "pos", "pbb")

            def is_psum(b):
                t = b[0] if isinstance(b, tuple) else b
                return t in PSUM_KEYS

            def add(dep, buf=None):
                if dep is None:
                    return
                k, v = dep
                if k == mykey and v == myval:
                    return
                if (not dma) and k == mykey and eng == "te" and buf is not None \
                        and is_psum(buf):
                    return
                if need.get(k, -1) < v:
                    need[k] = v
            for b in reads:
                add(last_w.get(b), b)
            for b in writes:
                add(last_w.get(b), b)
                for dep in readers.get(b, ()):
                    add(dep, None if not is_psum(b) else b)
            for b in writes:
                last_w[b] = (mykey, myval)
                readers[b] = []
            for b in reads:
                readers.setdefault(b, []).append((mykey, myval))
            plan.append((mykey, myval, need))

        sem_of = {("c", e): cs[e] for e in ENGS}
        for e in ("gq", "sq", "se"):
            sem_of[("d", e)] = ds[e]
        needed_te = set()
        for p in plan:
            if p[0] == "FENCE":
                v = p[1].get(("c", "te"))
                if v:
                    needed_te.add(v)
            else:
                v = p[2].get(("c", "te"))
                if v is not None:
                    needed_te.add(v)
        by_eng = {e: [i for i, o in enumerate(self.ops) if o[0] in (e, "FENCE")] for e in ENGS}
        ops = self.ops
        blk = es.enter_context(nc.Block())
        satisfied = {e: {} for e in ENGS}

        def make_body(eng_name):
            idxs = by_eng[eng_name]

            def body(eh):
                if eng_name == "gq" and regs_init is not None:
                    regs_init(eh)
                sat = satisfied[eng_name]
                for i in idxs:
                    if ops[i][0] == "FENCE":
                        totals = plan[i][1]
                        for k in sorted(totals, key=repr):
                            v = totals[k]
                            if v <= 0 or sat.get(k, 0) >= v:
                                continue
                            eh.wait_ge(sem_of[k], v)
                            sat[k] = v
                        continue
                    _, build, _, _, dma, _xb = ops[i]
                    mykey, myval, need = plan[i]
                    for k in sorted(need, key=repr):
                        v = need[k]
                        if sat.get(k, 0) >= v:
                            continue
                        eh.wait_ge(sem_of[k], v)
                        sat[k] = v
                    inst = build(eh)
                    if inst is None:
                        inst = eh.nop()
                    inst.then_inc(sem_of[mykey], 16 if dma else 1)
                if eng_name == final_waits_eng:
                    for e in ("se", "sq", "gq"):
                        if cnt_d[e] > 0:
                            eh.wait_ge(ds[e], cnt_d[e])

            return body

        blk.tensor(make_body("te"))
        blk.vector(make_body("ve"))
        blk.scalar(make_body("se"))
        blk.gpsimd(make_body("gq"))
        blk.sync(make_body("sq"))


def build_program():
    nc = bass.Bass("TRN2", target_bir_lowering=False)

    x_own = nc.dram_tensor("x_own", [TOK, H], F32, kind="ExternalInput")
    x_far = nc.dram_tensor("x_far", [TOK, H], F32, kind="ExternalInput")
    wq = nc.dram_tensor("wq", [H, H], F32, kind="ExternalInput")
    wk = nc.dram_tensor("wk", [H, DKV], F32, kind="ExternalInput")
    wv = nc.dram_tensor("wv", [H, DKV], F32, kind="ExternalInput")
    wo = nc.dram_tensor("wo", [H, H], F32, kind="ExternalInput")
    wT = nc.dram_tensor("wT", [H, H], F32, kind="ExternalInput")
    gwT = nc.dram_tensor("gwT", [H, 8], F32, kind="ExternalInput")
    bias1 = nc.dram_tensor("bias1", [1, H], F32, kind="ExternalInput")
    incs = nc.dram_tensor("incs", [8, H], F32, kind="ExternalInput")
    outcs = nc.dram_tensor("outcs", [8, H], F32, kind="ExternalInput")
    wmask = nc.dram_tensor("wmask", [128, 512], BF16, kind="ExternalInput")
    ident_in = nc.dram_tensor("ident_in", [128, 128], BF16, kind="ExternalInput")
    nonce_in = nc.dram_tensor("nonce_in", [1, 4], I32, kind="ExternalInput")
    out_d = nc.dram_tensor("out_d", [TOK, H], F32, kind="ExternalOutput")

    xn_own_d = nc.dram_tensor("xn_own_d", [TOK, H], BF16)
    xn_far_d = nc.dram_tensor("xn_far_d", [TOK, H], BF16)
    xs_d = nc.dram_tensor("xs_d", [H, TOK], BF16)
    attn_d = nc.dram_tensor("attn_d", [TOK, H], BF16)
    wTs_d = nc.dram_tensor("wTs_d", [H, H], BF16)
    kvK = nc.dram_tensor("kvK", [S, 1024], BF16, addr_space="Shared")
    kvK_st = nc.dram_tensor("kvK_st", [1024, 1024], BF16)
    kvV_st = nc.dram_tensor("kvV_st", [1024, 1024], BF16)
    kvV = nc.dram_tensor("kvV", [S, DKV], BF16, addr_space="Shared")
    meta_n = nc.dram_tensor("meta_n", [2, 512], F32, addr_space="Shared")
    meta_f = nc.dram_tensor("meta_f", [2, 128], I32, addr_space="Shared")

    P = Prog()
    es = ExitStack()
    SB = lambda name, shape, dt: es.enter_context(nc.sbuf_tensor(name, shape, dt))

    Qfm = [SB(f"Qfm{m}", [128, TOK], BF16) for m in range(32)]
    mask_sb = SB("mask_sb", [128, 512], BF16)
    id_sb = SB("id_sb", [128, 128], BF16)
    ones_col = SB("ones_col", [128, 1], BF16)
    ones_row = SB("ones_row", [1, 128], BF16)
    eps_sb = SB("eps_sb", [128, 1], F32)
    lnsc_sb = SB("lnsc_sb", [1, 1], F32)
    routT = SB("routT", [8, TOK], BF16)
    qn2 = SB("qn2", [1, 32], F32)
    kn_mine = SB("kn_mine", [1, 8], F32)
    kn_par = SB("kn_par", [1, 8], F32)
    nred = SB("nred", [1, 2], F32)
    pid_sb = SB("pid_sb", [1, 1], U32)
    nonce_sb = SB("nonce_sb", [1, 4], I32)
    flag_sb = SB("flag_sb", [1, 4], I32)

    def dma(q, out, in_, **kw):
        return lambda eh: eh.dma_start(out=out, in_=in_, **kw)

    P.op("gq", dma("gq", pid_sb[:], nc.partition_id_tensor[:]), writes=["pid"], dma=True)
    P.op("gq", dma("gq", nonce_sb[:], nonce_in[:]), writes=["nonce"], dma=True)
    P.op("sq", dma("sq", mask_sb[:], wmask[:]), writes=["mask"], dma=True)
    P.op("sq", dma("sq", id_sb[:], ident_in[:]), writes=["ident"], dma=True)
    P.op("ve", lambda eh: eh.memset(ones_col[:], 1.0), writes=["ones_col"])
    P.op("ve", lambda eh: eh.memset(ones_row[:], 1.0), writes=["ones_row"])
    P.op("ve", lambda eh: eh.memset(eps_sb[:], EPS), writes=["eps"])
    P.op("ve", lambda eh: eh.memset(lnsc_sb[:], math.log(SM_SCALE)), writes=["lnsc"])

    REG = {}

    def regs_init(g):
        names = ["pid", "r", "p", "s2", "vr0", "vr1",
                 "mn_own", "mn_par", "fl_own", "fl_par", "tmp",
                 "tgt", "nd", "fv", "nonce"]
        for n in names:
            REG[n] = es.enter_context(g.register(f"rg_{n}"))

    def regs_compute(g):
        R = REG
        g.reg_load(R["pid"], pid_sb[:1, :1])
        g.reg_load(R["nonce"], nonce_sb[:1, :1])
        g.reg_mod(R["r"], R["pid"], 4)
        g.reg_mod(R["p"], R["pid"], 2)
        g.reg_add(R["s2"], R["r"], 2)
        g.reg_mod(R["s2"], R["s2"], 4)
        g.reg_mov(R["vr0"], 0); g.reg_add(R["vr0"], R["vr0"], R["r"]); g.reg_mul(R["vr0"], R["vr0"], 1024 * 1024)
        g.reg_mov(R["vr1"], 0); g.reg_add(R["vr1"], R["vr1"], R["s2"]); g.reg_mul(R["vr1"], R["vr1"], 1024 * 1024)
        g.reg_mov(R["mn_own"], 0); g.reg_add(R["mn_own"], R["mn_own"], R["p"]); g.reg_mul(R["mn_own"], R["mn_own"], 512)
        g.reg_mov(R["mn_par"], 512); g.reg_sub(R["mn_par"], R["mn_par"], R["mn_own"])
        g.reg_mov(R["fl_own"], 0); g.reg_add(R["fl_own"], R["fl_own"], R["p"]); g.reg_mul(R["fl_own"], R["fl_own"], 128)
        g.reg_mov(R["fl_par"], 128); g.reg_sub(R["fl_par"], R["fl_par"], R["fl_own"])
        return None

    P.op("gq", regs_compute, reads=["pid", "nonce"], writes=["regs"])

    p0 = ExitStack()
    SB0 = lambda name, shape, dt: p0.enter_context(nc.sbuf_tensor(name, shape, dt))
    xt = [SB0(f"p0_x{j}", [128, H], F32) for j in range(2)]
    xnt = [SB0(f"p0_xn{j}", [128, H], BF16) for j in range(2)]
    st6 = [SB0(f"p0_st6{j}", [128, 48], F32) for j in range(2)]
    mv = [SB0(f"p0_mv{j}", [128, 2], F32) for j in range(2)]
    rstd = [SB0(f"p0_rs{j}", [128, 2], F32) for j in range(2)]
    wts_f = [SB0(f"p0_wtf{j}", [128, 2048], F32) for j in range(2)]
    wts_b = [SB0(f"p0_wtb{j}", [128, 2048], BF16) for j in range(2)]

    for t in range(16):
        src = x_own if t < 8 else x_far
        dstd = xn_own_d if t < 8 else xn_far_d
        dtag = "xnod" if t < 8 else "xnfd"
        row = (t % 8) * 128
        j = t % 2
        P.op("se", dma("se", xt[j][:], src[row:row + 128, :]), writes=[("xt", j)], dma=True)
        for i in range(8):
            P.op("ve", (lambda j=j, i=i: lambda eh: eh.bn_stats(
                out=st6[j][:, i * 6:(i + 1) * 6], in_=xt[j][:, i * 512:(i + 1) * 512]))(),
                reads=[("xt", j)], writes=[("st6", j)])
        P.op("ve", (lambda j=j: lambda eh: eh.bn_aggr(out=mv[j][:], in_=st6[j][:]))(),
             reads=[("st6", j)], writes=[("mv", j)])
        P.op("se", (lambda j=j: lambda eh: eh.activation(
            out=rstd[j][:, 0:1], in_=mv[j][:, 1:2], func=AF.Ln, bias=eps_sb[:, 0:1]))(),
            reads=[("mv", j), "eps"], writes=[("rstd", j)])
        P.op("se", (lambda j=j: lambda eh: eh.activation(
            out=rstd[j][:, 1:2], in_=rstd[j][:, 0:1], func=AF.Exp, scale=-0.5))(),
            reads=[("rstd", j)], writes=[("rstd", j)])
        P.op("ve", (lambda j=j: lambda eh: eh.tensor_scalar(
            out=xnt[j][:], in0=xt[j][:], scalar1=mv[j][:, 0:1], scalar2=rstd[j][:, 1:2],
            op0=ALU.subtract, op1=ALU.mult))(),
            reads=[("xt", j), ("mv", j), ("rstd", j)], writes=[("xnt", j)])
        P.op("se", dma("se", dstd[row:row + 128, :], xnt[j][:]),
             reads=[("xnt", j)], writes=[dtag], dma=True)

    for u in range(64):
        j = u % 2
        r0 = (u // 2) * 128
        c0 = (u % 2) * 2048
        P.op("sq", dma("sq", wts_f[j][:], wT[r0:r0 + 128, c0:c0 + 2048]),
             writes=[("wtsf", j)], dma=True)
        P.op("se", (lambda j=j: lambda eh: eh.sign(out=wts_b[j][:], in_=wts_f[j][:]))(),
             reads=[("wtsf", j)], writes=[("wtsb", j)])
        P.op("sq", dma("sq", wTs_d[r0:r0 + 128, c0:c0 + 2048], wts_b[j][:]),
             reads=[("wtsb", j)], writes=["wTs"], dma=True)
    p0.close()
    P.fence()

    p1 = ExitStack()
    SB1 = lambda name, shape, dt: p1.enter_context(nc.sbuf_tensor(name, shape, dt))
    xnfm = [SB1(f"xnfm{k}", [128, TOK], BF16) for k in range(32)]
    ps1 = ExitStack()
    pj = [ps1.enter_context(nc.psum_tensor(f"pj{i}", [128, 1024], F32)) for i in range(3)]
    pd = ps1.enter_context(nc.psum_tensor("pd", [128, 512], F32))
    peb = ps1.enter_context(nc.psum_tensor("peb", [128, 256], BF16))

    def xn_load(which):
        srcd = xn_own_d if which == "own" else xn_far_d
        srctag = "xnod" if which == "own" else "xnfd"
        for k in range(32):
            P.op("sq", dma("sq", xnfm[k][:], srcd[0:TOK, k * 128:(k + 1) * 128], transpose=True),
                 reads=[srctag], writes=[("xnfm", k)], dma=True, xbar=True)

    xn_load("own")

    WS = {}

    def open_wscope():
        st = ExitStack()
        WS["stack"] = st
        WS["wcol_f"] = st.enter_context(nc.sbuf_tensor(f"wcf{next(_ctr)}", [128, 8, 256], F32))
        WS["wcol_b"] = [st.enter_context(nc.sbuf_tensor(f"wcb{next(_ctr)}", [128, 32, 256], BF16)) for j in range(2)]
        WS["sqt"] = st.enter_context(nc.sbuf_tensor(f"sqt{next(_ctr)}", [128, 1024], BF16))
        WS["kevac"] = [st.enter_context(nc.sbuf_tensor(f"kev{next(_ctr)}", [128, 1024], BF16)) for j in range(2)]
        return st

    import itertools
    _ctr = itertools.count()

    def load_wcol_pair(wten, ncols, mp, j):
        for piece in range(4):
            k0 = piece * 8
            src = bass.AP(wten, (k0 * 128) * ncols + mp * 256,
                          [[ncols, 128], [128 * ncols, 8], [1, 256]])
            P.op("sq", (lambda src=src: lambda eh: eh.dma_start(
                out=WS["wcol_f"][:], in_=src))(),
                writes=["wcf"], dma=True)
            P.op("se", (lambda j=j, k0=k0: lambda eh: eh.copy(
                out=WS["wcol_b"][j][:, k0:k0 + 8, :], in_=WS["wcol_f"][:]))(),
                reads=["wcf"], writes=[("wcb", j)])

    def col_norm_max(src_bf, srckey, dst_ap, dstkey):
        sqt = WS["sqt"]
        P.op("se", (lambda sqt=sqt, src_bf=src_bf: lambda eh: eh.activation(
            out=sqt[:], in_=src_bf[:], func=AF.Square))(),
            reads=[srckey], writes=["sqt"])
        for half in range(2):
            P.op("te", (lambda half=half, sqt=sqt: lambda eh: eh.matmul(
                out=pd[0:1, 0:512], lhsT=ones_col[:], rhs=sqt[:, half * 512:(half + 1) * 512],
                start=True, stop=True))(),
                reads=["sqt", "ones_col"], writes=["pd"])
            P.op("ve", (lambda half=half: lambda eh: eh.reduce_max(
                out=nred[0:1, half:half + 1], in_=pd[0:1, 0:512], axis=AX.X))(),
                reads=["pd"], writes=["nred"])
        P.op("ve", lambda eh: eh.tensor_max(out=nred[0:1, 0:1], in0=nred[0:1, 0:1],
                                            in1=nred[0:1, 1:2]),
             reads=["nred"], writes=["nred"])
        P.op("ve", (lambda dst_ap=dst_ap: lambda eh: eh.tensor_max(
            out=dst_ap, in0=dst_ap, in1=nred[0:1, 0:1]))(),
            reads=["nred", dstkey], writes=[dstkey])

    P.op("ve", lambda eh: eh.memset(qn2[:], 0.0), writes=["qn2"])
    P.op("ve", lambda eh: eh.memset(kn_mine[:], 0.0), writes=["knm"])

    def q_proj():
        for mp in range(16):
            j = mp % 2
            load_wcol_pair(wq, H, mp, j)
            for mh in range(2):
                m = mp * 2 + mh
                slot = m % 3
                for k in range(32):
                    for half in range(2):
                        P.op("te", (lambda slot=slot, j=j, k=k, half=half, mh=mh: lambda eh: eh.matmul(
                            out=pj[slot][:, half * 512:(half + 1) * 512],
                            lhsT=WS["wcol_b"][j][:, k, mh * 128:(mh + 1) * 128],
                            rhs=xnfm[k][:, half * 512:(half + 1) * 512],
                            start=(k == 0), stop=(k == 31)))(),
                            reads=[("wcb", j), ("xnfm", k)], writes=[("pj", slot)])
                P.op("ve", (lambda m=m, slot=slot: lambda eh: eh.tensor_copy(
                    out=Qfm[m][:], in_=pj[slot][:]))(),
                    reads=[("pj", slot)], writes=[("Q", m)])
                col_norm_max(Qfm[m], ("Q", m), qn2[0:1, gmaj(m):gmaj(m) + 1], "qn2")

    def k_proj(pass_far):
        creg = "vr1" if pass_far else "vr0"
        for hk in range(HKV):
            if hk % 2 == 0:
                load_wcol_pair(wk, DKV, hk // 2, (hk // 2) % 2)
            j = (hk // 2) % 2
            slot = hk % 3
            for k in range(32):
                for half in range(2):
                    P.op("te", (lambda slot=slot, j=j, k=k, half=half, hh=hk % 2: lambda eh: eh.matmul(
                        out=pj[slot][:, half * 512:(half + 1) * 512],
                        lhsT=WS["wcol_b"][j][:, k, hh * 128:(hh + 1) * 128],
                        rhs=xnfm[k][:, half * 512:(half + 1) * 512],
                        start=(k == 0), stop=(k == 31)))(),
                        reads=[("wcb", j), ("xnfm", k)], writes=[("pj", slot)])
            je = hk % 2
            kv_t = WS["kevac"][je]
            P.op("ve", (lambda slot=slot, kv_t=kv_t: lambda eh: eh.tensor_copy(
                out=kv_t[:], in_=pj[slot][:]))(),
                reads=[("pj", slot)], writes=[("kev", je)])
            col_norm_max(kv_t, ("kev", je), kn_mine[0:1, hk:hk + 1], "knm")

            P.op("sq", (lambda hk=hk, kv_t=kv_t: lambda eh: eh.dma_start(
                out=kvK_st[hk * 128:(hk + 1) * 128, :], in_=kv_t[:]))(),
                reads=[("kev", je), "kvK_cp"], writes=["kvKst"], dma=True)

        def k_copy(eh, creg=creg):
            g = eh
            dst = bass.AP(kvK, REG[creg], [[1024, 1024], [1, 1024]])
            return g.dma_start(out=dst, in_=kvK_st[:])
        P.op("gq", k_copy, reads=["kvKst", "regs"], writes=["kvK_w", "kvK_cp"], dma=True)

    def v_proj(pass_far):
        vreg = "vr1" if pass_far else "vr0"
        p1b = ExitStack()
        tag = "f" if pass_far else "o"
        wvf = p1b.enter_context(nc.sbuf_tensor(f"wvf{tag}", [128, 8, 512], F32))
        wvb = p1b.enter_context(nc.sbuf_tensor(f"wvb{tag}", [128, 32, 512], BF16))
        vev8 = [p1b.enter_context(nc.sbuf_tensor(f"vev{tag}{mt}", [128, 1024], BF16)) for mt in range(8)]
        for half in range(2):
            for piece in range(4):
                k0 = piece * 8
                src = bass.AP(wv, (k0 * 128) * DKV + half * 512,
                              [[DKV, 128], [128 * DKV, 8], [1, 512]])
                P.op("sq", (lambda src=src: lambda eh: eh.dma_start(out=wvf[:], in_=src))(),
                     writes=["wvf"], dma=True)
                P.op("se", (lambda k0=k0: lambda eh: eh.copy(out=wvb[:, k0:k0 + 8, :], in_=wvf[:]))(),
                     reads=["wvf"], writes=["wvb"])
            for mt in range(8):
                slot = mt % 3
                for k in range(32):
                    P.op("te", (lambda slot=slot, k=k, mt=mt: lambda eh: eh.matmul(
                        out=pj[slot][:, 0:512],
                        lhsT=xnfm[k][:, mt * 128:(mt + 1) * 128],
                        rhs=wvb[:, k, :],
                        start=(k == 0), stop=(k == 31)))(),
                        reads=["wvb", ("xnfm", k)], writes=[("pj", slot)])
                P.op("ve", (lambda slot=slot, mt=mt, half=half: lambda eh: eh.tensor_copy(
                    out=vev8[mt][:, half * 512:(half + 1) * 512], in_=pj[slot][:, 0:512]))(),
                    reads=[("pj", slot)], writes=[("vev", mt)])
        for mt in range(8):
            P.op("sq", (lambda mt=mt, vt=vev8[mt]: lambda eh: eh.dma_start(
                out=kvV_st[mt * 128:(mt + 1) * 128, :], in_=vt[:]))(),
                reads=[("vev", mt), "kvV_cp"], writes=["kvVst"], dma=True)

        def v_copy(eh, vreg=vreg):
            g = eh
            dst = bass.AP(kvV, REG[vreg], [[1024, 1024], [1, 1024]])
            return g.dma_start(out=dst, in_=kvV_st[:])
        P.op("gq", v_copy, reads=["kvVst", "regs"], writes=["kvV_w", "kvV_cp"], dma=True)
        p1b.close()
        P.fence()

    ws = open_wscope()
    q_proj()
    k_proj(False)
    ws.close()
    P.fence()
    v_proj(False)

    p1c = ExitStack()
    gwb = p1c.enter_context(nc.sbuf_tensor("gwb", [128, 32, 8], BF16))
    rtmp = p1c.enter_context(nc.sbuf_tensor("rtmp", [128, 12], F32))
    rbf = p1c.enter_context(nc.sbuf_tensor("rbf", [128, 8], BF16))
    gsrc = bass.AP(gwT, 0, [[8, 128], [1024, 32], [1, 8]])
    P.op("gq", dma("gq", gwb[:], gsrc, allow_slow_non_contiguous=True), writes=["gwb"], dma=True)
    for m in range(8):
        for k in range(32):
            P.op("te", (lambda m=m, k=k: lambda eh: eh.matmul(
                out=pd[:, m * 64:m * 64 + 8], lhsT=xnfm[k][:, m * 128:(m + 1) * 128],
                rhs=gwb[:, k, :],
                start=(m == 0 and k == 0), stop=(m == 7 and k == 31),
                skip_group_check=True))(),
                reads=["gwb", ("xnfm", k)], writes=["pd"])
    for m in range(8):
        P.op("ve", (lambda m=m: lambda eh: eh.reduce_max(
            out=rtmp[:, 8:9], in_=pd[:, m * 64:m * 64 + 8], axis=AX.X, negate=True))(),
            reads=["pd"], writes=["rtmp"])
        P.op("se", (lambda m=m: lambda eh: eh.activation(
            out=rtmp[:, 0:8], in_=pd[:, m * 64:m * 64 + 8], func=AF.Exp,
            bias=rtmp[:, 8:9]))(),
            reads=["pd", "rtmp"], writes=["rtmp"])
        P.op("ve", lambda eh: eh.reduce_sum(out=rtmp[:, 9:10], in_=rtmp[:, 0:8], axis=AX.X),
             reads=["rtmp"], writes=["rtmp"])
        P.op("ve", lambda eh: eh.reciprocal(out=rtmp[:, 10:11], in_=rtmp[:, 9:10]),
             reads=["rtmp"], writes=["rtmp"])
        P.op("ve", lambda eh: eh.tensor_scalar_mul(out=rbf[:], in0=rtmp[:, 0:8],
                                                   scalar1=rtmp[:, 10:11]),
             reads=["rtmp"], writes=["rbf"])
        P.op("te", lambda eh: eh.transpose(out=peb[0:8, 0:128], in_=rbf[:], identity=id_sb[:]),
             reads=["rbf", "ident"], writes=["peb"])
        P.op("ve", (lambda m=m: lambda eh: eh.tensor_copy(
            out=routT[0:8, m * 128:(m + 1) * 128], in_=peb[0:8, 0:128]))(),
            reads=["peb"], writes=["routT"])

    incs_b = p1c.enter_context(nc.sbuf_tensor("incs_b", [8, H], BF16))
    xst = [p1c.enter_context(nc.sbuf_tensor(f"xst{j}", [128, TOK], BF16)) for j in range(2)]
    P.op("gq", dma("gq", incs_b[:], incs[:]), writes=["incsb"], dma=True)
    for k in range(32):
        slot = k % 3
        for half in range(2):
            P.op("te", (lambda slot=slot, k=k, half=half: lambda eh: eh.matmul(
                out=pj[slot][:, half * 512:(half + 1) * 512],
                lhsT=incs_b[0:8, k * 128:(k + 1) * 128],
                rhs=routT[0:8, half * 512:(half + 1) * 512],
                start=True, stop=True))(),
                reads=["incsb", "routT"], writes=[("pj", slot)])
        j = k % 2
        P.op("ve", (lambda slot=slot, j=j, k=k: lambda eh: eh.tensor_tensor(
            out=xst[j][:], in0=xnfm[k][:], in1=pj[slot][:], op=ALU.mult))(),
            reads=[("pj", slot), ("xnfm", k)], writes=[("xst", j)])
        P.op("se", dma("se", xs_d[k * 128:(k + 1) * 128, :], xst[j][:]),
             reads=[("xst", j)], writes=["xs_d"], dma=True)
    p1c.close()
    P.fence()

    xn_load("far")
    ws = open_wscope()
    k_proj(True)
    ws.close()
    P.fence()
    v_proj(True)
    p1.close()
    ps1.close()

    def meta_write(eh):
        dst = bass.AP(meta_n, REG["mn_own"], [[1, 1], [1, 8]])
        return eh.dma_start(out=dst, in_=kn_mine[0:1, 0:8])
    P.op("gq", meta_write, reads=["knm", "regs", "kvK_w", "kvV_w"], writes=["meta_w"], dma=True)

    def flag_write(eh):
        dst = bass.AP(meta_f, REG["fl_own"], [[1, 1], [1, 1]])
        return eh.dma_start(out=dst, in_=nonce_sb[0:1, 0:1])
    P.op("gq", flag_write, reads=["meta_w", "nonce", "regs"], writes=["flag_w"], dma=True)

    poll_sem = es.enter_context(nc.semaphore("poll_sem"))

    def barrier(eh):
        g = eh
        R = REG
        g.reg_mov(R["tgt"], 16)
        g.reg_mov(R["nd"], 1)
        with g.While(R["nd"]):
            src = bass.AP(meta_f, R["fl_par"], [[1, 1], [1, 1]])
            g.dma_start(out=flag_sb[0:1, 0:1], in_=src).then_inc(poll_sem, 16)
            g.wait_ge(poll_sem, R["tgt"])
            g.reg_add(R["tgt"], R["tgt"], 16)
            g.reg_mov(R["nd"], 0)
            g.reg_load(R["fv"], flag_sb[0:1, 0:1])
            with g.If_ne(R["fv"], R["nonce"]):
                g.reg_mov(R["nd"], 1)
        return None
    P.op("gq", barrier, reads=["flag_w", "regs"], writes=["BARRIER"])

    def par_read(eh):
        src = bass.AP(meta_n, REG["mn_par"], [[1, 1], [1, 8]])
        return eh.dma_start(out=kn_par[0:1, 0:8], in_=src)
    P.op("gq", par_read, reads=["BARRIER", "regs"], writes=["knp"], dma=True)

    P.fence()
    p2 = ExitStack()
    SB2 = lambda name, shape, dt: p2.enter_context(nc.sbuf_tensor(name, shape, dt))
    attn_tm = SB2("atm", [128, NQT, H], BF16)
    Kb = [SB2(f"Kb{j}", [128, S], BF16) for j in range(2)]
    Vb = [SB2(f"Vb{j}", [128, 32, 130], BF16) for j in range(2)]
    probs = [SB2(f"probs{j}", [128, 1024], BF16) for j in range(2)]
    negc = SB2("negc", [128, 32], F32)
    kn_all = SB2("kn_all", [1, 8], F32)
    prodv = SB2("prodv", [1, 32], F32)
    negc_bf = SB2("negc_bf", [1, 32], BF16)
    rinv = SB2("rinv", [128, 8], F32)
    ps2 = ExitStack()
    psc = [ps2.enter_context(nc.psum_tensor(f"psc{j}", [128, 1024], F32)) for j in range(2)]
    ppv = [ps2.enter_context(nc.psum_tensor(f"ppv{j}", [128, 512], F32)) for j in range(3)]
    PV_SLOT = {0: (0, 0), 1: (0, 136), 2: (0, 272), 3: (1, 0), 4: (1, 136), 5: (1, 272),
               6: (2, 0), 7: (2, 136)}

    for j in range(2):
        P.op("ve", (lambda j=j: lambda eh: eh.memset(Vb[j][:, :, 128:130], 0.0))(),
             writes=[("Vb", j)])
        P.op("ve", (lambda j=j: lambda eh: eh.memset(Vb[j][:, :, 128:129], 1.0))(),
             writes=[("Vb", j)])

    P.op("ve", lambda eh: eh.tensor_max(out=kn_all[:], in0=kn_mine[0:1, 0:8],
                                        in1=kn_par[0:1, 0:8]),
         reads=["knm", "knp"], writes=["kn_all"])
    for g_ in range(4):
        P.op("ve", (lambda g_=g_: lambda eh: eh.tensor_mul(
            out=prodv[0:1, g_ * 8:(g_ + 1) * 8], in0=qn2[0:1, g_ * 8:(g_ + 1) * 8],
            in1=kn_all[0:1, 0:8]))(),
            reads=["qn2", "kn_all"], writes=["prodv"])
    P.op("se", lambda eh: eh.activation(out=prodv[:], in_=prodv[:], func=AF.Ln),
         reads=["prodv"], writes=["prodv"])
    P.op("se", lambda eh: eh.activation(out=prodv[:], in_=prodv[:], func=AF.Exp,
                                        scale=0.5, bias=lnsc_sb[0:1, 0:1]),
         reads=["prodv", "lnsc"], writes=["prodv"])
    P.op("ve", lambda eh: eh.tensor_scalar_mul(out=negc_bf[:], in0=prodv[:], scalar1=-1.0),
         reads=["prodv"], writes=["negc_bf"])
    P.op("te", lambda eh: eh.matmul(out=psc[0][:, 0:32], lhsT=ones_row[:], rhs=negc_bf[:],
                                    start=True, stop=True),
         reads=["negc_bf", "ones_row"], writes=[("psc", 0)])
    P.op("ve", lambda eh: eh.tensor_copy(out=negc[:], in_=psc[0][:, 0:32]),
         reads=[("psc", 0)], writes=["negc"])

    for hk in range(HKV):
        j = hk % 2
        for sres in range(4):
            P.op("gq", (lambda hk=hk, j=j, sres=sres: lambda eh: eh.dma_start(
                out=Kb[j][:, sres * 1024:(sres + 1) * 1024],
                in_=kvK[sres * 1024 + hk * 128: sres * 1024 + (hk + 1) * 128, :]))(),
                reads=["BARRIER"], writes=[("Kb", j)], dma=True)
        for quad in range(4):
            srcv = bass.AP(kvV, (quad * 1024) * DKV + hk * 128,
                           [[DKV, 128], [128 * DKV, 8], [1, 128]])
            P.op("gq", (lambda srcv=srcv, j=j, quad=quad: lambda eh: eh.dma_start(
                out=Vb[j][:, quad * 8:(quad + 1) * 8, 0:128], in_=srcv))(),
                reads=["BARRIER"], writes=[("Vb", j)], dma=True)
        for gg in range(GQ):
            h = hk * GQ + gg
            for v in range(32):
                i0 = v % 8
                N = (8 - i0) * 128
                jp = v % 2
                for half in range((N + 511) // 512):
                    w = min(512, N - half * 512)
                    P.op("te", (lambda jp=jp, j=j, v=v, h=h, i0=i0, half=half, w=w:
                                lambda eh: eh.matmul(
                        out=psc[jp][:, half * 512:half * 512 + w],
                        lhsT=Kb[j][:, v * 128:(v + 1) * 128],
                        rhs=Qfm[h][:, i0 * 128 + half * 512: i0 * 128 + half * 512 + w],
                        start=True, stop=True))(),
                        reads=[("Kb", j), ("Q", h)], writes=[("psc", jp)])
                P.op("se", (lambda jp=jp, N=N, h=h: lambda eh: eh.activation(
                    out=probs[jp][:, 0:N], in_=psc[jp][:, 0:N], func=AF.Exp,
                    bias=negc[:, gmaj(h):gmaj(h) + 1], scale=SM_SCALE))(),
                    reads=[("psc", jp), "negc"], writes=[("probs", jp)])
                s_idx = v // 8
                P.op("ve", (lambda jp=jp, s_idx=s_idx: lambda eh: eh.tensor_tensor(
                    out=probs[jp][:, 0:128], in0=probs[jp][:, 0:128],
                    in1=mask_sb[:, s_idx * 128:(s_idx + 1) * 128], op=ALU.mult))(),
                    reads=[("probs", jp), "mask"], writes=[("probs", jp)])
                for i in range(i0, 8):
                    bank, off = PV_SLOT[i]
                    st = (v == 0) and (i in (0, 3, 6))
                    P.op("te", (lambda jp=jp, j=j, v=v, i=i, i0=i0, bank=bank, off=off, st=st:
                                lambda eh: eh.matmul(
                        out=ppv[bank][:, off:off + 130],
                        lhsT=probs[jp][:, (i - i0) * 128:(i - i0 + 1) * 128],
                        rhs=Vb[j][:, v, 0:130],
                        start=st, stop=(v == 24 + i), skip_group_check=True))(),
                        reads=[("probs", jp), ("Vb", j)], writes=[("ppv", bank)])
            for i in range(NQT):
                bank, off = PV_SLOT[i]
                P.op("ve", (lambda bank=bank, off=off, i=i: lambda eh: eh.reciprocal(
                    out=rinv[:, i:i + 1], in_=ppv[bank][:, off + 128:off + 129]))(),
                    reads=[("ppv", bank)], writes=["rinv"])
                P.op("ve", (lambda bank=bank, off=off, i=i, h=h: lambda eh: eh.tensor_scalar_mul(
                    out=attn_tm[:, i, h * 128:(h + 1) * 128],
                    in0=ppv[bank][:, off:off + 128], scalar1=rinv[:, i:i + 1]))(),
                    reads=[("ppv", bank), "rinv"], writes=[("atm", i)])
            dsth = bass.AP(attn_d, h * 128, [[H, 128], [128 * H, 8], [1, 128]])
            P.op("se", (lambda dsth=dsth, h=h: lambda eh: eh.dma_start(
                out=dsth, in_=attn_tm[:, :, h * 128:(h + 1) * 128]))(),
                reads=[("atm", i) for i in range(NQT)], writes=[("attn_d", h)], dma=True)
    ps2.close()
    p2.close()
    P.fence()

    p3 = ExitStack()
    SB3 = lambda name, shape, dt: p3.enter_context(nc.sbuf_tensor(name, shape, dt))
    xsfm = [SB3(f"xsfm{k}", [128, TOK], BF16) for k in range(32)]
    wo_st = SB3("wo_st", [128, 32, NB], BF16)
    wt_st = SB3("wt_st", [128, 32, NB], BF16)
    wo_f = [SB3(f"wo_f{j}", [128, 2, NB], F32) for j in range(2)]
    s_sb = [SB3(f"s_sb{m}", [128, NB], BF16) for m in range(8)]
    bias_bc = SB3("bias_bc", [128, H], BF16)
    ocs_b = SB3("ocs_b", [8, H], BF16)
    _ys = SB3("ys_sb0", [128, NB], F32); ys_sb = [_ys, _ys]
    _t1 = SB3("t1_sb0", [128, NB], F32); t1_sb = [_t1, _t1]
    _g = SB3("g_sb0", [128, NB], BF16); g_sb = [_g, _g]
    _h = SB3("h_sb0", [128, NB], BF16); h_sb = [_h, _h]
    xe = [SB3("xe0", [128, 4, NB], F32)]
    oe = [SB3("oe0", [128, 4, NB], F32)]
    p3b = ExitStack()
    bias_b = p3b.enter_context(nc.sbuf_tensor("bias_b", [1, H], BF16))
    ps3 = ExitStack()
    pp = [ps3.enter_context(nc.psum_tensor(f"pp{i}", [128, 512], F32)) for i in range(6)]
    pos = ps3.enter_context(nc.psum_tensor("pos", [128, 512], F32))
    pbb = ps3.enter_context(nc.psum_tensor("pbb", [128, 512], F32))

    for k in range(32):
        P.op("sq", (lambda k=k: lambda eh: eh.dma_start(
            out=Qfm[k][:], in_=attn_d[0:TOK, k * 128:(k + 1) * 128], transpose=True))(),
            reads=[("attn_d", k)], writes=[("Q", k)], dma=True, xbar=True)
    for k in range(32):
        P.op("sq", (lambda k=k: lambda eh: eh.dma_start(
            out=xsfm[k][:], in_=xs_d[k * 128:(k + 1) * 128, :]))(),
            reads=["xs_d"], writes=[("xs", k)], dma=True)
    P.op("gq", dma("gq", bias_b[:], bias1[:]), writes=["biasb"], dma=True)
    for n8 in range(8):
        P.op("te", (lambda n8=n8: lambda eh: eh.matmul(
            out=pbb[:, 0:512], lhsT=ones_row[:], rhs=bias_b[0:1, n8 * 512:(n8 + 1) * 512],
            start=True, stop=True))(),
            reads=["biasb", "ones_row"], writes=["pbb"])
        P.op("ve", (lambda n8=n8: lambda eh: eh.tensor_copy(
            out=bias_bc[:, n8 * 512:(n8 + 1) * 512], in_=pbb[:, 0:512]))(),
            reads=["pbb"], writes=["biasbc"])
    P.op("gq", dma("gq", ocs_b[:], outcs[:]), writes=["ocsb"], dma=True)
    P.fence()
    p3b.close()

    for n in range(NNB):
        for piece in range(16):
            j = piece % 2
            k0 = piece * 2
            src = bass.AP(wo, (k0 * 128) * H + n * NB,
                          [[H, 128], [128 * H, 2], [1, NB]])
            P.op("sq", (lambda j=j, src=src: lambda eh: eh.dma_start(
                out=wo_f[j][:], in_=src))(),
                writes=[("wo_f", j)], dma=True)
            P.op("se", (lambda j=j, k0=k0: lambda eh: eh.copy(
                out=wo_st[:, k0:k0 + 2, :], in_=wo_f[j][:]))(),
                reads=[("wo_f", j)], writes=["wo_st"])
        for m in range(8):
            slot = m % 6
            for k in range(32):
                P.op("te", (lambda slot=slot, m=m, k=k: lambda eh: eh.matmul(
                    out=pp[slot][:, 0:NB], lhsT=Qfm[k][:, m * 128:(m + 1) * 128],
                    rhs=wo_st[:, k, :], start=(k == 0), stop=(k == 31)))(),
                    reads=[("Q", k), "wo_st"], writes=[("pp", slot)])
            P.op("ve", (lambda slot=slot, m=m: lambda eh: eh.tensor_copy(
                out=s_sb[m][:], in_=pp[slot][:, 0:NB]))(),
                reads=[("pp", slot)], writes=[("s", m)])
        for piece in range(4):
            k0 = piece * 8
            src = bass.AP(wTs_d, (k0 * 128) * H + n * NB,
                          [[H, 128], [128 * H, 8], [1, NB]])
            P.op("sq", (lambda src=src, k0=k0: lambda eh: eh.dma_start(
                out=wt_st[:, k0:k0 + 8, :], in_=src))(),
                reads=["wTs"], writes=["wt_st"], dma=True)
        for m in range(8):
            slot = m % 6
            for k in range(32):
                P.op("te", (lambda slot=slot, m=m, k=k: lambda eh: eh.matmul(
                    out=pp[slot][:, 0:NB], lhsT=xsfm[k][:, m * 128:(m + 1) * 128],
                    rhs=wt_st[:, k, :], start=(k == 0), stop=(k == 31)))(),
                    reads=[("xs", k), "wt_st"], writes=[("pp", slot)])
            j = m % 2
            P.op("ve", (lambda slot=slot, j=j: lambda eh: eh.tensor_copy(
                out=ys_sb[j][:], in_=pp[slot][:, 0:NB]))(),
                reads=[("pp", slot)], writes=[("ys", 0)])
            P.op("te", (lambda m=m, n=n: lambda eh: eh.matmul(
                out=pos[:, 0:NB], lhsT=routT[0:8, m * 128:(m + 1) * 128],
                rhs=ocs_b[0:8, n * NB:(n + 1) * NB], start=True, stop=True))(),
                reads=["routT", "ocsb"], writes=["pos"])
            P.op("ve", (lambda j=j: lambda eh: eh.tensor_tensor(
                out=t1_sb[j][:], in0=ys_sb[j][:], in1=pos[:, 0:NB], op=ALU.mult))(),
                reads=[("ys", 0), "pos"], writes=[("t1", 0)])
            P.op("ve", (lambda j=j, n=n: lambda eh: eh.tensor_tensor(
                out=t1_sb[j][:], in0=t1_sb[j][:], in1=bias_bc[:, n * NB:(n + 1) * NB],
                op=ALU.add))(),
                reads=[("t1", 0), "biasbc"], writes=[("t1", 0)])
            P.op("se", (lambda j=j: lambda eh: eh.activation(
                out=g_sb[j][:], in_=t1_sb[j][:], func=AF.Silu))(),
                reads=[("t1", 0)], writes=[("g", 0)])
            P.op("ve", (lambda j=j, m=m: lambda eh: eh.tensor_tensor(
                out=h_sb[j][:], in0=g_sb[j][:], in1=s_sb[m][:], op=ALU.mult))(),
                reads=[("g", 0), ("s", m)], writes=[("h", 0)])
            if m % 4 == 0:
                xsrc = bass.AP(x_own, (m * 128) * H + n * NB,
                               [[H, 128], [128 * H, 4], [1, NB]])
                P.op("se", (lambda xsrc=xsrc: lambda eh: eh.dma_start(
                    out=xe[0][:], in_=xsrc))(),
                    writes=[("xe", 0)], dma=True)
            P.op("ve", (lambda j=j, m=m: lambda eh: eh.tensor_tensor(
                out=oe[0][:, m % 4, :], in0=h_sb[j][:], in1=xe[0][:, m % 4, :],
                op=ALU.add))(),
                reads=[("h", 0), ("xe", 0)], writes=[("oe", 0)])
            if m % 4 == 3:
                odst = bass.AP(out_d, ((m - 3) * 128) * H + n * NB,
                               [[H, 128], [128 * H, 4], [1, NB]])
                P.op("se", (lambda odst=odst: lambda eh: eh.dma_start(
                    out=odst, in_=oe[0][:]))(),
                    reads=[("oe", 0)], writes=["out_d"], dma=True)
    ps3.close()
    p3.close()

    if os.environ.get("KDBG") == "1":
        dbg_xn = nc.dram_tensor("dbg_xn", [TOK, H], BF16, kind="ExternalOutput")
        dbg_xs = nc.dram_tensor("dbg_xs", [H, TOK], BF16, kind="ExternalOutput")
        dbg_at = nc.dram_tensor("dbg_at", [TOK, H], BF16, kind="ExternalOutput")
        P.fence()
        P.op("gq", dma("gq", dbg_xn[:], xn_own_d[:]), reads=["xnod"], writes=["dbg"], dma=True)
        P.op("gq", dma("gq", dbg_xs[:], xs_d[:]), reads=["xs_d"], writes=["dbg"], dma=True)
        P.op("gq", dma("gq", dbg_at[:], attn_d[:]), reads=[("attn_d", 0)], writes=["dbg"], dma=True)
        dbg_rt = nc.dram_tensor("dbg_rt", [8, TOK], BF16, kind="ExternalOutput")
        P.op("se", dma("se", dbg_rt[:], routT[0:8, :]), reads=["routT"], writes=["dbg"], dma=True)

    P.emit(nc, es, regs_init, final_waits_eng="se")
    return nc


def kernel(L0_attn_hidden_states, L0_attn_w_q, L0_attn_w_k, L0_attn_w_v, L0_attn_w_o,
           L0_ffn_weight, L0_ffn_bias, L0_ffn_gate_w, L0_ffn_in_channel_scale,
           L0_ffn_out_channel_scale):
    hs = np.ascontiguousarray(np.asarray(L0_attn_hidden_states, dtype=np.float32))
    wq = np.ascontiguousarray(np.asarray(L0_attn_w_q, dtype=np.float32))
    wk = np.ascontiguousarray(np.asarray(L0_attn_w_k, dtype=np.float32))
    wv = np.ascontiguousarray(np.asarray(L0_attn_w_v, dtype=np.float32))
    wo = np.ascontiguousarray(np.asarray(L0_attn_w_o, dtype=np.float32))
    wT = np.ascontiguousarray(np.asarray(L0_ffn_weight, dtype=np.float32).T)
    gwT = np.ascontiguousarray(np.asarray(L0_ffn_gate_w, dtype=np.float32).T)
    bias1 = np.ascontiguousarray(np.asarray(L0_ffn_bias, dtype=np.float32).reshape(1, H))
    incs = np.ascontiguousarray(np.asarray(L0_ffn_in_channel_scale, dtype=np.float32))
    outcs = np.ascontiguousarray(np.asarray(L0_ffn_out_channel_scale, dtype=np.float32))

    hs_t = hs.reshape(B, 8, 4, 128, H)
    ident = np.eye(128, dtype=np.float32).astype(ml_dtypes.bfloat16)
    nv = (int.from_bytes(os.urandom(4), 'little') & 0x7FFFFFFF) | 1
    nonce = np.full((1, 4), nv, dtype=np.int32)

    kk = np.arange(128)[:, None]
    qq = np.arange(128)[None, :]
    in_maps = []
    for c in range(8):
        b, r = c // 4, c % 4
        x_own_a = np.ascontiguousarray(hs_t[b, :, r].reshape(TOK, H))
        x_far_a = np.ascontiguousarray(hs_t[b, :, (r + 2) % 4].reshape(TOK, H))
        wm = np.zeros((128, 512), dtype=np.float32)
        for s in range(4):
            allowed = (kk - qq) <= (r - s) * 128
            wm[:, s * 128:(s + 1) * 128] = allowed.astype(np.float32)
        in_maps.append({
            "x_own": x_own_a, "x_far": x_far_a, "wq": wq, "wk": wk, "wv": wv, "wo": wo,
            "wT": wT, "gwT": gwT, "bias1": bias1, "incs": incs, "outcs": outcs,
            "wmask": wm.astype(ml_dtypes.bfloat16), "ident_in": ident,
            "nonce_in": nonce,
        })

    nc = build_program()
    res = run_bass_kernel_spmd(nc, in_maps, core_ids=list(range(8)))
    out = np.empty((B, S, H), dtype=np.float32)
    out_t = out.reshape(B, 8, 4, 128, H)
    for c in range(8):
        b, r = c // 4, c % 4
        out_t[b, :, r] = res.results[c]["out_d"].reshape(8, 128, H)
    return out
